# version 05
# speedup vs baseline: 1.3840x; 1.0133x over previous
import contextlib
import numpy as np
import concourse.bass as bass
import concourse.mybir as mybir
from concourse.bass_utils import run_bass_kernel_spmd
from concourse.bass_core_rust import IndirectOffsetOnAxis

F32 = mybir.dt.float32
BF16 = mybir.dt.bfloat16
I32 = mybir.dt.int32
AF = mybir.ActivationFunctionType
ALU = mybir.AluOpType
AX = mybir.AxisListType

T = 2048
D = 1024
NT = T // 128
ND = D // 128
MAGIC = float(np.float32(1.5 * 2 ** 23))
EPS_LN = 1e-5

L0_H, L0_G, L0_DQ, L0_DV = 16, 4, 32, 64
INV_SQRT_DQ = float(1.0 / np.sqrt(np.float32(32.0)))
SLOPES = [2.0 ** (-8.0 * (h + 1) / 16.0) for h in range(L0_H)]

L1_H = 16
N_EXP = 8
N_TOK_GLOBAL = 8 * T
CAPACITY = int(1.2 * (N_TOK_GLOBAL // N_EXP + 1))
CAP_PAD = 2560
NTE = CAP_PAD // 128
MOE_EPS = 1e-9


class Ctr:
    def __init__(self, sem):
        self.sem = sem
        self.v = 0

    def inc(self, inst, n=1):
        self.v += n
        inst.then_inc(self.sem, n)
        return self.v

    def wait(self, eng, v=None):
        eng.wait_ge(self.sem, self.v if v is None else v)


def build_program(dbg=(), stages=None):
    nc = bass.Bass("TRN2", target_bir_lowering=False)

    def want(name):
        return stages is None or name in stages

    def dram(name, shape, dtype):
        kind = "ExternalOutput" if name in dbg else None
        if kind:
            return nc.dram_tensor(name, shape, dtype, kind=kind)
        return nc.dram_tensor(name, shape, dtype)

    x_in = nc.dram_tensor("x", [T, D], F32, kind="ExternalInput").ap()
    y_out = nc.dram_tensor("y", [T, D], F32, kind="ExternalOutput").ap()
    w_l0wq = nc.dram_tensor("l0wq", [D, 512], F32, kind="ExternalInput").ap()
    w_l0wk = nc.dram_tensor("l0wk", [D, 128], F32, kind="ExternalInput").ap()
    w_l0wv = nc.dram_tensor("l0wv", [D, 256], F32, kind="ExternalInput").ap()
    w_l0wp = nc.dram_tensor("l0wp", [D, D], F32, kind="ExternalInput").ap()
    w_l0bp = nc.dram_tensor("l0bp", [1, D], F32, kind="ExternalInput").ap()
    c_posk = nc.dram_tensor("posk", [5, T], BF16, kind="ExternalInput").ap()
    c_slope = nc.dram_tensor("slope4", [L0_H, 5, T], BF16, kind="ExternalInput").ap()
    c_onesblk = nc.dram_tensor("onesblk", [128, 4], F32, kind="ExternalInput").ap()
    w_l0w1t = nc.dram_tensor("l0w1t", [D, 4096], F32, kind="ExternalInput").ap()
    w_l0cw = nc.dram_tensor("l0cw", [4096, 3], F32, kind="ExternalInput").ap()
    w_l0cb = nc.dram_tensor("l0cb", [128, 32], F32, kind="ExternalInput").ap()
    w_l0w2t = nc.dram_tensor("l0w2t", [4096, D], F32, kind="ExternalInput").ap()
    w_l1wqt = nc.dram_tensor("l1wqt", [D, D], F32, kind="ExternalInput").ap()
    w_l1wkt = nc.dram_tensor("l1wkt", [D, D], F32, kind="ExternalInput").ap()
    w_l1wvt = nc.dram_tensor("l1wvt", [D, D], F32, kind="ExternalInput").ap()
    w_l1wot = nc.dram_tensor("l1wot", [D, D], F32, kind="ExternalInput").ap()
    w_l1bq = nc.dram_tensor("l1bq", [128, ND], F32, kind="ExternalInput").ap()
    w_l1bk = nc.dram_tensor("l1bk", [128, ND], F32, kind="ExternalInput").ap()
    w_l1bv = nc.dram_tensor("l1bv", [1, D], F32, kind="ExternalInput").ap()
    w_l1bo = nc.dram_tensor("l1bo", [1, D], F32, kind="ExternalInput").ap()
    w_l1src = nc.dram_tensor("l1src", [1, 4], I32, kind="ExternalInput").ap()
    w_l1tgt = nc.dram_tensor("l1tgt", [1, 4], I32, kind="ExternalInput").ap()
    c_ones64 = nc.dram_tensor("ones64", [128, 2], F32, kind="ExternalInput").ap()
    w_exprows = nc.dram_tensor("exprows", [128, 20], I32, kind="ExternalInput").ap()
    w_owncols = nc.dram_tensor("owncols", [64, 1], I32, kind="ExternalInput").ap()
    w_tv2 = nc.dram_tensor("tv2", [128, 32], F32, kind="ExternalInput").ap()
    w_tokrows1 = nc.dram_tensor("tokrows1", [128, NT], I32, kind="ExternalInput").ap()
    w_tokrows2 = nc.dram_tensor("tokrows2", [128, NT], I32, kind="ExternalInput").ap()
    w_fc1t = nc.dram_tensor("fc1t", [D, 4096], F32, kind="ExternalInput").ap()
    w_fc1b = nc.dram_tensor("fc1b", [128, 32], F32, kind="ExternalInput").ap()
    w_fc2t = nc.dram_tensor("fc2t", [4096, D], F32, kind="ExternalInput").ap()
    w_fc2b = nc.dram_tensor("fc2b", [1, D], F32, kind="ExternalInput").ap()
    w_ls = nc.dram_tensor("ls", [1, D], F32, kind="ExternalInput").ap()
    w_rwt = nc.dram_tensor("rwt", [D, 8], F32, kind="ExternalInput").ap()
    w_rb = nc.dram_tensor("rb", [1, 8], F32, kind="ExternalInput").ap()
    c_utri = nc.dram_tensor("utri", [128, 128], BF16, kind="ExternalInput").ap()
    c_usf = nc.dram_tensor("usf", [128, 128], F32, kind="ExternalInput").ap()
    c_mask01 = nc.dram_tensor("mask01", [128, 128], BF16, kind="ExternalInput").ap()

    d_xn0fm = dram("d_xn0fm", [D, T], F32)
    d_xn0fm_bf = dram("d_xn0fm_bf", [D, T], BF16)
    d_rows0 = dram("d_rows0", [4, T], F32)
    d_ctx_tm = dram("d_ctx_tm", [T, D], BF16)
    d_x1p = dram("d_x1p", [T, D], F32)
    d_w1q = dram("d_w1q", [D, 4096], BF16)
    d_w2q = dram("d_w2q", [4096, D], BF16)
    d_h = dram("d_h", [4096, T], F32)
    d_xq2 = dram("d_xq2", [4096, T], BF16)
    d_x1 = dram("d_x1", [T, D], F32)
    d_rows2 = dram("d_rows2", [2, T], F32)
    d_xn1tm = dram("d_xn1tm", [T, D], BF16)
    d_xn1lo = dram("d_xn1lo", [T, D], BF16)
    d_q1fm = dram("d_q1fm", [D, T], BF16)
    d_k1fm = dram("d_k1fm", [D, T], BF16)
    d_v1tm = dram("d_v1tm", [T, D], BF16)
    d_q1tm = dram("d_q1tm", [T, D], BF16)
    d_k1tm = dram("d_k1tm", [T, D], BF16)
    d_l1acc = dram("d_l1acc", [L1_H * 65, T], F32)
    d_attn1 = dram("d_attn1", [T, D], F32)
    c_xn_in = nc.dram_tensor("c_xn_in", [T, D], BF16)
    c_lg_in = nc.dram_tensor("c_lg_in", [T, 8], F32)
    c_xf_out = nc.dram_tensor("c_xf_out", [8 * T, D], BF16)
    c_lg_out = nc.dram_tensor("c_lg_out", [8 * T, 8], F32)
    d_T = dram("d_T", [8 * T * 2, 2], F32)
    d_tok = dram("d_tok", [CAP_PAD, D], BF16)
    d_ymdbg = dram("d_ymdbg", [CAP_PAD, D], BF16) if "d_ymdbg" in dbg else None
    d_cydbg = dram("d_cydbg", [8 * CAP_PAD, D], BF16) if "d_cydbg" in dbg else None
    d_iodbg = dram("d_iodbg", [128, 64], F32) if "d_iodbg" in dbg else None
    d_hmoe = dram("d_hmoe", [4096, CAP_PAD], BF16)
    d_ymoe = dram("d_ymoe", [CAP_PAD, D], BF16)
    c_y_out = nc.dram_tensor("c_y_out", [8 * CAP_PAD, D], BF16)
    d_lgdbg = dram("d_lgdbg", [8 * T, 8], F32) if "d_lgdbg" in dbg else None
    d_dbg7 = dram("d_dbg7", [128, 4096], F32) if "d_dbg7" in dbg else None
    d_G = nc.dram_tensor("d_G", [161 * 128, 2], F32)
    c_G_out = nc.dram_tensor("c_G_out", [161 * 128, 2], F32)
    d_wtT = nc.dram_tensor("d_wtT", [512, 128], F32)
    d_Gdbg = dram("d_Gdbg", [N_EXP * CAP_PAD, 2], F32) if "d_Gdbg" in dbg else None
    d_csrow = dram("d_csrow", [8, 4], F32)
    d_dbg = dram("d_dbg", [128, 128], F32)

    CONSTS = [EPS_LN]

    ctx = contextlib.ExitStack()
    with ctx:
        ident = ctx.enter_context(nc.sbuf_tensor("ident", [128, 128], F32))
        ccx = ctx.enter_context(nc.semaphore("ccx"))
        ones1f = ctx.enter_context(nc.sbuf_tensor("ones1f", [1, 128], F32))
        ones1b = ctx.enter_context(nc.sbuf_tensor("ones1b", [1, 128], BF16))
        scale1_row = ctx.enter_context(nc.sbuf_tensor("scale1_row", [1, T], F32))
        recip1_row = ctx.enter_context(nc.sbuf_tensor("recip1_row", [1, T], F32))
        cpool = ctx.enter_context(nc.sbuf_tensor("cpool", [128, max(1, len(CONSTS))], F32))
        cmap = {}

        def cst(v):
            return cpool[:, cmap[float(v)]:cmap[float(v)] + 1]

        with nc.Block() as b:
            @b.gpsimd
            def _(gp):
                gp.memset(ident[:], 1.0)
                gp.affine_select(ident[:], ident[:], [[1, 128]], ALU.is_equal, 0.0,
                                 base=0, channel_multiplier=-1)
                gp.memset(ones1f[:], 1.0)
                gp.memset(ones1b[:], 1.0)
                for j, v in enumerate(CONSTS):
                    cmap[float(v)] = j
                    gp.memset(cpool[:, j:j + 1], float(v))
        nc.all_engine_barrier()

        if want("s0"):
            with contextlib.ExitStack() as st:
                xsb = st.enter_context(nc.sbuf_tensor("s0_x", [128, NT * D], F32))
                wsb = st.enter_context(nc.sbuf_tensor("s0_w", [128, NT * D], F32))
                s1 = st.enter_context(nc.sbuf_tensor("s0_s1", [128, NT], F32))
                s2 = st.enter_context(nc.sbuf_tensor("s0_s2", [128, NT], F32))
                mm = st.enter_context(nc.sbuf_tensor("s0_m", [128, NT], F32))
                rs = st.enter_context(nc.sbuf_tensor("s0_rs", [128, NT], F32))
                amx = st.enter_context(nc.sbuf_tensor("s0_amx", [128, NT], F32))
                art = st.enter_context(nc.sbuf_tensor("s0_art", [16, 128], F32))
                fmf = st.enter_context(nc.sbuf_tensor("s0_fmf", [128, 256], F32))
                fmb = st.enter_context(nc.sbuf_tensor("s0_fmb", [128, 256], BF16))
                pst0 = st.enter_context(nc.psum_tensor("s0_ps0", [128, 512], F32))
                pst1 = st.enter_context(nc.psum_tensor("s0_ps1", [128, 512], F32))
                psr = st.enter_context(nc.psum_tensor("s0_psr", [16, 128], F32))
                ld = Ctr(st.enter_context(nc.semaphore("s0ld")))
                fn = Ctr(st.enter_context(nc.semaphore("s0fn")))

                with nc.Block() as b:
                    @b.sync
                    def _(sync):
                        ld.inc(sync.dma_start(out=xsb[:].rearrange("p (i d) -> p i d", d=D),
                                              in_=x_in[:, :].rearrange("(i p) d -> p i d", p=128)), 16)

                    @b.scalar
                    def _(s):
                        ld.wait(s)
                        for i in range(NT):
                            s.activation(wsb[:, i * D:(i + 1) * D], xsb[:, i * D:(i + 1) * D], AF.Square)
                nc.all_engine_barrier()

                with nc.Block() as b:
                    @b.vector
                    def _(v):
                        v.reduce_sum(s1[:], xsb[:].rearrange("p (i d) -> p i d", d=D), AX.X)
                        v.reduce_sum(s2[:], wsb[:].rearrange("p (i d) -> p i d", d=D), AX.X)
                        v.tensor_scalar(mm[:], s1[:], 1.0 / D, None, ALU.mult)
                        v.tensor_scalar(rs[:], s2[:], 1.0 / D, None, ALU.mult)
                        fn.inc(v.tensor_tensor(s1[:], mm[:], mm[:], ALU.mult))
                        fn.wait(v)
                        v.tensor_sub(s2[:], rs[:], s1[:])
                nc.all_engine_barrier()
                if "d_dbg" in dbg:
                    with nc.Block() as b:
                        @b.sync
                        def _(sync):
                            sync.dma_start(out=d_dbg.ap()[:, 96:112], in_=s2[:]).then_inc(ld.sem, 16)
                            sync.dma_start(out=d_dbg.ap()[:, 112:113], in_=cpool[:, 0:1], allow_slow_non_contiguous=True).then_inc(ld.sem, 16)
                            ld.v += 32
                            ld.wait(sync)
                    nc.all_engine_barrier()

                with nc.Block() as b:
                    @b.scalar
                    def _(s):
                        s.activation(s2[:], s2[:], AF.Sqrt, bias=cst(EPS_LN))
                nc.all_engine_barrier()

                with nc.Block() as b:
                    @b.vector
                    def _(v):
                        fn.inc(v.reciprocal(rs[:], s2[:]))
                        fn.wait(v)
                        for i in range(NT):
                            v.tensor_scalar(wsb[:, i * D:(i + 1) * D],
                                            xsb[:, i * D:(i + 1) * D],
                                            mm[:, i:i + 1], rs[:, i:i + 1],
                                            ALU.subtract, ALU.mult)
                        v.reduce_max(amx[:], wsb[:].rearrange("p (i d) -> p i d", d=D),
                                     AX.X, apply_absolute_value=True)
                nc.all_engine_barrier()
                if "d_dbg" in dbg:
                    with nc.Block() as b:
                        @b.sync
                        def _(sync):
                            sync.dma_start(out=d_dbg.ap()[:, 0:16], in_=s1[:]).then_inc(ld.sem, 16)
                            sync.dma_start(out=d_dbg.ap()[:, 16:32], in_=s2[:]).then_inc(ld.sem, 16)
                            sync.dma_start(out=d_dbg.ap()[:, 80:96], in_=wsb[:, 0:16]).then_inc(ld.sem, 16)
                            sync.dma_start(out=d_dbg.ap()[:, 32:48], in_=mm[:]).then_inc(ld.sem, 16)
                            sync.dma_start(out=d_dbg.ap()[:, 48:64], in_=rs[:]).then_inc(ld.sem, 16)
                            sync.dma_start(out=d_dbg.ap()[:, 64:80], in_=amx[:]).then_inc(ld.sem, 16)
                            ld.v += 96
                            ld.wait(sync)
                    nc.all_engine_barrier()

                with nc.Block() as b:
                    @b.tensor
                    def _(t):
                        t.transpose(psr[:], amx[:], ident[:])
                nc.all_engine_barrier()
                with nc.Block() as b:
                    @b.vector
                    def _(v):
                        v.tensor_copy(art[:], psr[:])
                nc.all_engine_barrier()
                with nc.Block() as b:
                    @b.sync
                    def _(sync):
                        ld.inc(sync.dma_start(out=d_rows0.ap()[3:4, :].rearrange("o (i p) -> (o i) p", p=128),
                                              in_=art[:]), 16)
                        ld.wait(sync)
                        ld.inc(sync.dma_start(out=scale1_row[:], in_=d_rows0.ap()[3:4, :]), 16)

                    @b.vector
                    def _(v):
                        ld.wait(v)
                        fn.inc(v.tensor_scalar(scale1_row[:], scale1_row[:], 1e-5, None, ALU.max))
                        fn.wait(v)
                        fn.inc(v.tensor_scalar(recip1_row[:], scale1_row[:], 1.0 / 127.0, None, ALU.mult))
                        fn.wait(v)
                        fn.inc(v.reciprocal(scale1_row[:], scale1_row[:]))
                        fn.wait(v)
                        v.tensor_scalar(scale1_row[:], scale1_row[:], 127.0, None, ALU.mult)
                nc.all_engine_barrier()
                with nc.Block() as b:
                    @b.sync
                    def _(sync):
                        sync.dma_start(out=d_rows0.ap()[0:1, :], in_=scale1_row[:]).then_inc(ld.sem, 16)
                        sync.dma_start(out=d_rows0.ap()[1:2, :], in_=recip1_row[:]).then_inc(ld.sem, 16)
                        ld.v += 32
                        ld.wait(sync)
                nc.all_engine_barrier()

                tp = Ctr(st.enter_context(nc.semaphore("s0tp")))
                evf = Ctr(st.enter_context(nc.semaphore("s0ev")))
                cvb = Ctr(st.enter_context(nc.semaphore("s0cv")))
                out_q = Ctr(st.enter_context(nc.semaphore("s0oq")))
                with nc.Block() as b:
                    @b.tensor
                    def _(t):
                        k = 0
                        for i in range(NT):
                            for c in range(ND):
                                if k >= 2:
                                    t.wait_ge(evf.sem, k - 1)
                                pb = pst0 if k % 2 == 0 else pst1
                                tp.inc(t.transpose(pb[:, 0:128],
                                                   wsb[:, i * D + c * 128: i * D + c * 128 + 128],
                                                   ident[:]))
                                k += 1

                    @b.scalar
                    def _(s):
                        k = 0
                        for i in range(NT):
                            for c in range(ND):
                                s.wait_ge(tp.sem, k + 1)
                                if k >= 2:
                                    s.wait_ge(out_q.sem, (k - 2) * 32 + 16)
                                    s.wait_ge(cvb.sem, k - 1)
                                pb = pst0 if k % 2 == 0 else pst1
                                evf.inc(s.activation(fmf[:, (k % 2) * 128:(k % 2) * 128 + 128],
                                                     pb[:, 0:128], AF.Copy))
                                k += 1

                    @b.vector
                    def _(v):
                        k = 0
                        for i in range(NT):
                            for c in range(ND):
                                v.wait_ge(evf.sem, k + 1)
                                if k >= 2:
                                    v.wait_ge(out_q.sem, (k - 2) * 32 + 32)
                                cvb.inc(v.tensor_copy(fmb[:, (k % 2) * 128:(k % 2) * 128 + 128],
                                                      fmf[:, (k % 2) * 128:(k % 2) * 128 + 128]))
                                k += 1

                    @b.gpsimd
                    def _(gp):
                        k = 0
                        for i in range(NT):
                            for c in range(ND):
                                gp.wait_ge(evf.sem, k + 1)
                                out_q.inc(gp.dma_start(
                                    out=d_xn0fm.ap()[c * 128:(c + 1) * 128, i * 128:(i + 1) * 128],
                                    in_=fmf[:, (k % 2) * 128:(k % 2) * 128 + 128]), 16)
                                gp.wait_ge(cvb.sem, k + 1)
                                out_q.inc(gp.dma_start(
                                    out=d_xn0fm_bf.ap()[c * 128:(c + 1) * 128, i * 128:(i + 1) * 128],
                                    in_=fmb[:, (k % 2) * 128:(k % 2) * 128 + 128]), 16)
                                k += 1
                        gp.wait_ge(out_q.sem, out_q.v)
                nc.all_engine_barrier()


        if want("s1"):
            with contextlib.ExitStack() as st:
                xnb = st.enter_context(nc.sbuf_tensor("s1_xnb", [128, ND * T], BF16))
                wq = st.enter_context(nc.sbuf_tensor("s1_wq", [128, ND * 512], BF16))
                wk = st.enter_context(nc.sbuf_tensor("s1_wk", [128, ND * 128], BF16))
                wv = st.enter_context(nc.sbuf_tensor("s1_wv", [128, ND * 256], BF16))
                qsb = st.enter_context(nc.sbuf_tensor("s1_q", [128, 4 * T], BF16))
                ksb = st.enter_context(nc.sbuf_tensor("s1_k", [128, T], BF16))
                vbuf = st.enter_context(nc.sbuf_tensor("s1_vbuf", [128, NT * 260], BF16))
                qbuf = st.enter_context(nc.sbuf_tensor("s1_qbuf", [37, L0_H * T], BF16))
                kbuf = st.enter_context(nc.sbuf_tensor("s1_kbuf", [37, 4 * T], BF16))
                sqs = st.enter_context(nc.sbuf_tensor("s1_sqs", [128, T], BF16))
                onesb = st.enter_context(nc.sbuf_tensor("s1_onesb", [128, 4], BF16))
                nrm = st.enter_context(nc.sbuf_tensor("s1_nrm", [4, 16], F32))
                csr = st.enter_context(nc.sbuf_tensor("s1_csr", [4, 4], F32))
                crow = st.enter_context(nc.sbuf_tensor("s1_crow", [1, 16], F32))
                negM = st.enter_context(nc.sbuf_tensor("s1_negM", [128, 16], F32))
                negMg = st.enter_context(nc.sbuf_tensor("s1_negMg", [128, 4], F32))
                mask01 = st.enter_context(nc.sbuf_tensor("s1_mask01", [128, 128], BF16))
                mask4 = st.enter_context(nc.sbuf_tensor("s1_mask4", [128, 512], BF16))
                expT = st.enter_context(nc.sbuf_tensor("s1_expT", [128, 4 * 512], BF16))
                rcp4 = st.enter_context(nc.sbuf_tensor("s1_rcp4", [128, 4], F32))
                ctxtb = st.enter_context(nc.sbuf_tensor("s1_ctxtb", [128, D], BF16))
                ctxS = st.enter_context(nc.sbuf_tensor("s1_ctxS", [128, 512], F32))
                psS0 = st.enter_context(nc.psum_tensor("s1_psS0", [128, 512], F32))
                psS1 = st.enter_context(nc.psum_tensor("s1_psS1", [128, 512], F32))
                psS2 = st.enter_context(nc.psum_tensor("s1_psS2", [128, 512], F32))
                psA = st.enter_context(nc.psum_tensor("s1_psA", [128, 512], F32))
                psA1 = st.enter_context(nc.psum_tensor("s1_psA1", [128, 512], F32))
                psN = st.enter_context(nc.psum_tensor("s1_psN", [128, 512], F32))
                psB0 = st.enter_context(nc.psum_tensor("s1_psB0", [128, 512], F32))
                psB1 = st.enter_context(nc.psum_tensor("s1_psB1", [128, 512], F32))
                ld = Ctr(st.enter_context(nc.semaphore("s1ld")))
                pj = Ctr(st.enter_context(nc.semaphore("s1pj")))
                ev = Ctr(st.enter_context(nc.semaphore("s1ev")))
                fn = Ctr(st.enter_context(nc.semaphore("s1fn")))

                with nc.Block() as b:
                    @b.sync
                    def _(sync):
                        ld.inc(sync.dma_start(out=xnb[:].rearrange("p (c t) -> p c t", t=T),
                                              in_=d_xn0fm_bf.ap().rearrange("(c p) t -> p c t", p=128)), 16)

                    @b.gpsimd
                    def _(gp):
                        ld.inc(gp.dma_start(out=wq[:].rearrange("p (c o) -> p c o", o=512),
                                            in_=w_l0wq.rearrange("(c p) o -> p c o", p=128)), 16)
                        ld.inc(gp.dma_start(out=wk[:].rearrange("p (c o) -> p c o", o=128),
                                            in_=w_l0wk.rearrange("(c p) o -> p c o", p=128)), 16)
                        ld.inc(gp.dma_start(out=wv[:].rearrange("p (c o) -> p c o", o=256),
                                            in_=w_l0wv.rearrange("(c p) o -> p c o", p=128)), 16)
                        ld.inc(gp.dma_start(out=onesb[:], in_=c_onesblk[:, :]), 16)
                        ld.inc(gp.dma_start(out=mask01[:], in_=c_mask01[:, :]), 16)
                        for j4 in range(4):
                            ld.inc(gp.dma_start(out=mask4[:, j4 * 128:(j4 + 1) * 128],
                                                in_=c_mask01[:, :]), 16)
                        gp.memset(vbuf[:].rearrange("p (i g c) -> p i g c", g=4, c=65)[:, :, :, 64:65], 1.0)
                        ld.wait(gp)
                nc.all_engine_barrier()

                NS = T // 512
                with nc.Block() as b:
                    @b.tensor
                    def _(t):
                        k = 0
                        for oc in range(4):
                            for ts in range(NS):
                                if k >= 2:
                                    t.wait_ge(ev.sem, k - 1)
                                pb = psS0 if k % 2 == 0 else psS1
                                for dc in range(ND):
                                    mmi = t.matmul(pb[:, 0:512],
                                                   wq[:, dc * 512 + oc * 128: dc * 512 + oc * 128 + 128],
                                                   xnb[:, dc * T + ts * 512: dc * T + ts * 512 + 512],
                                                   start=(dc == 0), stop=(dc == ND - 1))
                                pj.inc(mmi)
                                k += 1
                        for ts in range(NS):
                            if k >= 2:
                                t.wait_ge(ev.sem, k - 1)
                            pb = psS0 if k % 2 == 0 else psS1
                            for dc in range(ND):
                                mmi = t.matmul(pb[:, 0:512],
                                               wk[:, dc * 128: dc * 128 + 128],
                                               xnb[:, dc * T + ts * 512: dc * T + ts * 512 + 512],
                                               start=(dc == 0), stop=(dc == ND - 1))
                            pj.inc(mmi)
                            k += 1
                        for tt in range(NT):
                            if k >= 2:
                                t.wait_ge(ev.sem, k - 1)
                            pb = psS0 if k % 2 == 0 else psS1
                            for dc in range(ND):
                                mmi = t.matmul(pb[:, 0:256],
                                               xnb[:, dc * T + tt * 128: dc * T + tt * 128 + 128],
                                               wv[:, dc * 256: dc * 256 + 256],
                                               start=(dc == 0), stop=(dc == ND - 1))
                            pj.inc(mmi)
                            k += 1

                    @b.vector
                    def _(v):
                        k = 0
                        for oc in range(4):
                            for ts in range(NS):
                                v.wait_ge(pj.sem, k + 1)
                                pb = psS0 if k % 2 == 0 else psS1
                                ev.inc(v.tensor_copy(qsb[:, oc * T + ts * 512: oc * T + ts * 512 + 512],
                                                     pb[:, 0:512]))
                                k += 1
                        for ts in range(NS):
                            v.wait_ge(pj.sem, k + 1)
                            pb = psS0 if k % 2 == 0 else psS1
                            ev.inc(v.tensor_copy(ksb[:, ts * 512: ts * 512 + 512], pb[:, 0:512]))
                            k += 1
                        for tt in range(NT):
                            v.wait_ge(pj.sem, k + 1)
                            pb = psS0 if k % 2 == 0 else psS1
                            cp = None
                            for g in range(4):
                                cp = v.tensor_copy(vbuf[:, tt * 260 + g * 65: tt * 260 + g * 65 + 64],
                                                   pb[:, g * 64: g * 64 + 64])
                            ev.inc(cp)
                            k += 1
                nc.all_engine_barrier()

                with nc.Block() as b:
                    @b.sync
                    def _(sync):
                        for g in range(4):
                            for j in range(4):
                                ld.inc(sync.dma_start(
                                    out=qbuf[0:32, g * 4 * T:(g + 1) * 4 * T]
                                    .rearrange("p (q j c) -> p q j c", j=4, c=128)[:, :, j, :],
                                    in_=qsb[32 * j:32 * j + 32, g * T:(g + 1) * T]
                                    .rearrange("p (q c) -> p q c", c=128)), 16)
                        for g in range(4):
                            ld.inc(sync.dma_start(out=kbuf[0:32, g * T:(g + 1) * T],
                                                  in_=ksb[32 * g:32 * g + 32, :]), 16)

                    @b.gpsimd
                    def _(gp):
                        for g in range(4):
                            for j in range(4):
                                h = g + 4 * j
                                ld.inc(gp.dma_start(
                                    out=qbuf[32:37, g * 4 * T:(g + 1) * 4 * T]
                                    .rearrange("p (q j c) -> p q j c", j=4, c=128)[:, :, j, :],
                                    in_=c_slope[h, :, :].rearrange("p (q c) -> p q c", c=128)), 16)
                        for g in range(4):
                            ld.inc(gp.dma_start(out=kbuf[32:37, g * T:(g + 1) * T],
                                                in_=c_posk[:, :]), 16)
                        ld.wait(gp)
                nc.all_engine_barrier()

                psmL = [psS0, psS1, psA, psN]
                for src in range(5):
                    with nc.Block() as b:
                        @b.scalar
                        def _(s):
                            if src == 4:
                                s.activation(sqs[:], ksb[:], AF.Square)
                            else:
                                s.activation(sqs[:], qsb[:, src * T:(src + 1) * T], AF.Square)
                    nc.all_engine_barrier()
                    with nc.Block() as b:
                        @b.tensor
                        def _(t):
                            for ts in range(NS):
                                t.matmul(psmL[ts][0:4, 0:512], onesb[:],
                                         sqs[:, ts * 512:(ts + 1) * 512], start=True, stop=True)
                    nc.all_engine_barrier()
                    with nc.Block() as b:
                        @b.vector
                        def _(v):
                            for ts in range(NS):
                                rr = v.reduce_max(nrm[0:4, 4 * (src % 2) + ts: 4 * (src % 2) + ts + 1],
                                                  psmL[ts][0:4, 0:512], AX.X)
                            fn.inc(rr)
                            fn.wait(v)
                            rr2 = v.reduce_max(nrm[0:4, 8 + src: 8 + src + 1],
                                               nrm[0:4, 4 * (src % 2): 4 * (src % 2) + 4], AX.X)
                            fn.inc(rr2)
                            fn.wait(v)
                    nc.all_engine_barrier()
                with nc.Block() as b:
                    @b.vector
                    def _(v):
                        pass
                nc.all_engine_barrier()
                with nc.Block() as b:
                    @b.sync
                    def _(sync):
                        ld.inc(sync.dma_start(out=d_csrow.ap()[0:1, :], in_=nrm[0:4, 12:13],
                                              allow_slow_non_contiguous=True), 16)
                        ld.wait(sync)
                        ld.inc(sync.dma_start(out=nrm[0:1, 13:14].rearrange("a b -> a b") if False else crow[0:1, 0:4],
                                              in_=d_csrow.ap()[0:1, 0:4]), 16)
                        ld.wait(sync)
                nc.all_engine_barrier()
                with nc.Block() as b:
                    @b.vector
                    def _(v):
                        pass
                nc.all_engine_barrier()
                with nc.Block() as b:
                    @b.tensor
                    def _(t):
                        t.matmul(psN[0:4, 0:4], ones1f[:, 0:4], crow[0:1, 0:4], start=True, stop=True)
                nc.all_engine_barrier()
                with nc.Block() as b:
                    @b.vector
                    def _(v):
                        fn.inc(v.tensor_tensor(csr[0:4, :], nrm[0:4, 8:12], psN[0:4, 0:4], ALU.mult))
                        fn.wait(v)
                nc.all_engine_barrier()
                with nc.Block() as b:
                    @b.scalar
                    def _(s):
                        s.activation(csr[0:4, :], csr[0:4, :], AF.Sqrt)
                nc.all_engine_barrier()
                with nc.Block() as b:
                    @b.sync
                    def _(sync):
                        ld.inc(sync.dma_start(out=d_csrow.ap()[1:5, :], in_=csr[0:4, :],
                                              allow_slow_non_contiguous=True), 16)
                        ld.wait(sync)
                        ld.inc(sync.dma_start(out=crow[0:1, 0:16],
                                              in_=d_csrow.ap().rearrange("a b -> (a b)")[None, 4:20]), 16)

                    @b.vector
                    def _(v):
                        ld.wait(v)
                        v.tensor_scalar(crow[0:1, 0:16], crow[0:1, 0:16],
                                        -1.02 * INV_SQRT_DQ, None, ALU.mult)
                nc.all_engine_barrier()
                with nc.Block() as b:
                    @b.tensor
                    def _(t):
                        t.matmul(psN[:, 0:16], ones1f[:], crow[0:1, 0:16], start=True, stop=True)
                nc.all_engine_barrier()
                with nc.Block() as b:
                    @b.vector
                    def _(v):
                        fn.inc(v.tensor_copy(negM[:], psN[:, 0:16]))
                        fn.wait(v)
                        fn.inc(v.tensor_reduce(negMg[:], negM[:].rearrange("p (j g) -> p g j", g=4),
                                               AX.X, ALU.min))
                        fn.wait(v)
                nc.all_engine_barrier()

                mmS = Ctr(st.enter_context(nc.semaphore("s2mm")))
                exF = Ctr(st.enter_context(nc.semaphore("s2ex")))
                mkF = Ctr(st.enter_context(nc.semaphore("s2mk")))
                cxF = Ctr(st.enter_context(nc.semaphore("s2cx")))
                evF = Ctr(st.enter_context(nc.semaphore("s2ev")))
                dmF = Ctr(st.enter_context(nc.semaphore("s2dm")))

                evC = Ctr(st.enter_context(nc.semaphore("s2evC")))
                trT = Ctr(st.enter_context(nc.semaphore("s2tr")))
                dvF = Ctr(st.enter_context(nc.semaphore("s2dv")))

                sched = []
                for qi in range(NT):
                    for g in range(4):
                        for ki in range(qi + 1):
                            sched.append((qi, g, ki))
                sgrp = {key: i + 1 for i, key in enumerate(sched)}
                ggrp = {}
                ng = 0
                for qi in range(NT):
                    for g in range(4):
                        ng += 1
                        ggrp[(qi, g)] = ng
                NGRP = ng

                def prev_user_chain():
                    last = {0: None, 1: None, 2: None, 3: None}
                    out = {}
                    for key in sched:
                        par = key[2] % 4
                        out[key] = last[par]
                        last[par] = key
                    return out
                prev_bank = prev_user_chain()
                SBANKS = None

                with nc.Block() as b:
                    @b.tensor
                    def _(t):
                        def emit_tr(n):
                            t.wait_ge(evC.sem, n)
                            for j in range(4):
                                ti = 4 * (n - 1) + j + 1
                                if ti > 2:
                                    t.wait_ge(dvF.sem, ti - 2)
                                pbT = psB0 if ti % 2 == 1 else psB1
                                trT.inc(t.transpose(pbT[0:128, 0:65],
                                                    ctxS[0:65, j * 128:(j + 1) * 128],
                                                    ident[0:65, 0:65]))
                        pend = None
                        for (qi, g, ki) in sched:
                            n = ggrp[(qi, g)]
                            pv = prev_bank[(qi, g, ki)]
                            if pv is not None:
                                t.wait_ge(exF.sem, sgrp[pv])
                            pb = [psS0, psS1, psS2, psN][ki % 4]
                            mmS.inc(t.matmul(pb[:, 0:512],
                                             kbuf[:, g * T + ki * 128: g * T + ki * 128 + 128],
                                             qbuf[:, g * 4 * T + qi * 512: g * 4 * T + qi * 512 + 512],
                                             start=True, stop=True))
                            if ki == qi:
                                t.wait_ge(mkF.sem, n)
                            else:
                                t.wait_ge(exF.sem, sgrp[(qi, g, ki)])
                            if ki == 0 and n > 2:
                                t.wait_ge(evC.sem, n - 2)
                            pA = psA if n % 2 == 1 else psA1
                            cxF.inc(t.matmul(pA[0:65, 0:512],
                                             vbuf[:, ki * 260 + g * 65: ki * 260 + g * 65 + 65],
                                             expT[:, (ki % 4) * 512: (ki % 4) * 512 + 512],
                                             start=(ki == 0), stop=(ki == qi)))
                            if ki == qi:
                                if pend is not None:
                                    emit_tr(pend)
                                pend = n
                        emit_tr(pend)

                    @b.scalar
                    def _(s):
                        for (qi, g, ki) in sched:
                            s.wait_ge(mmS.sem, sgrp[(qi, g, ki)])
                            pv = prev_bank[(qi, g, ki)]
                            if pv is not None:
                                s.wait_ge(cxF.sem, sgrp[pv])
                            pb = [psS0, psS1, psS2, psN][ki % 4]
                            exF.inc(s.activation(expT[:, (ki % 4) * 512: (ki % 4) * 512 + 512],
                                                 pb[:, 0:512], AF.Exp,
                                                 bias=negMg[:, g:g + 1],
                                                 scale=INV_SQRT_DQ))

                    @b.vector
                    def _(v):
                        def emit_div(n):
                            qi2, g2 = (n - 1) // 4, (n - 1) % 4
                            if g2 == 0 and qi2 >= 1:
                                v.wait_ge(dmF.sem, 16 * qi2)
                            for j in range(4):
                                ti = 4 * (n - 1) + j + 1
                                v.wait_ge(trT.sem, ti)
                                pbT = psB0 if ti % 2 == 1 else psB1
                                h = g2 + 4 * j
                                fn.inc(v.reciprocal(rcp4[:, j:j + 1], pbT[0:128, 64:65]))
                                fn.wait(v)
                                dvF.inc(v.tensor_scalar(ctxtb[:, h * 64: h * 64 + 64],
                                                        pbT[0:128, 0:64],
                                                        rcp4[:, j:j + 1], None, ALU.mult))
                        for qi in range(NT):
                            for g in range(4):
                                n = ggrp[(qi, g)]
                                v.wait_ge(exF.sem, sgrp[(qi, g, qi)])
                                par = qi % 4
                                mkF.inc(v.tensor_mul(expT[:, par * 512: par * 512 + 512],
                                                     expT[:, par * 512: par * 512 + 512],
                                                     mask4[:]))
                                if n >= 2:
                                    emit_div(n - 1)
                                v.wait_ge(cxF.sem, sgrp[(qi, g, qi)])
                                if n >= 2:
                                    v.wait_ge(trT.sem, 4 * (n - 1))
                                pA = psA if n % 2 == 1 else psA1
                                evC.inc(v.tensor_copy(ctxS[0:65, :], pA[0:65, 0:512]))
                        emit_div(NGRP)

                    @b.gpsimd
                    def _(gp):
                        for qi in range(NT):
                            gp.wait_ge(dvF.sem, 16 * (qi + 1))
                            dmF.inc(gp.dma_start(out=d_ctx_tm.ap()[qi * 128:(qi + 1) * 128, :],
                                                 in_=ctxtb[:]), 16)
                        gp.wait_ge(dmF.sem, dmF.v)
                nc.all_engine_barrier()

        if want("s3"):
            with contextlib.ExitStack() as st:
                cfm = st.enter_context(nc.sbuf_tensor("s3_cfm", [128, ND * T], BF16))
                wp = st.enter_context(nc.sbuf_tensor("s3_wp", [128, ND * D], BF16))
                bpr = st.enter_context(nc.sbuf_tensor("s3_bp", [1, D], BF16))
                x0t = st.enter_context(nc.sbuf_tensor("s3_x0", [128, D], F32))
                o1 = st.enter_context(nc.sbuf_tensor("s3_o1", [128, D], F32))
                ps0 = st.enter_context(nc.psum_tensor("s3_ps0", [128, 512], F32))
                ps1 = st.enter_context(nc.psum_tensor("s3_ps1", [128, 512], F32))
                ld = Ctr(st.enter_context(nc.semaphore("s3ld")))
                pj = Ctr(st.enter_context(nc.semaphore("s3pj")))
                ev = Ctr(st.enter_context(nc.semaphore("s3ev")))
                dm = Ctr(st.enter_context(nc.semaphore("s3dm")))

                with nc.Block() as b:
                    @b.sync
                    def _(sync):
                        for c in range(ND):
                            ld.inc(sync.dma_start_transpose(cfm[:, c * T:(c + 1) * T],
                                                            d_ctx_tm.ap()[:, c * 128:(c + 1) * 128]), 16)

                    @b.gpsimd
                    def _(gp):
                        ld.inc(gp.dma_start(out=wp[:].rearrange("p (c o) -> p c o", o=D),
                                            in_=w_l0wp.rearrange("(c p) o -> p c o", p=128)), 16)
                        ld.inc(gp.dma_start(out=bpr[:], in_=w_l0bp[:, :]), 16)
                        ld.wait(gp)
                nc.all_engine_barrier()

                base_ld = ld.v
                with nc.Block() as b:
                    @b.sync
                    def _(sync):
                        for tt in range(NT):
                            if tt >= 1:
                                sync.wait_ge(ev.sem, tt * 2)
                            sync.dma_start(out=x0t[:], in_=x_in[tt * 128:(tt + 1) * 128, :]).then_inc(ld.sem, 16)
                            ld.v += 16

                    @b.tensor
                    def _(t):
                        for tt in range(NT):
                            for os in range(2):
                                k = tt * 2 + os
                                if k >= 2:
                                    t.wait_ge(ev.sem, k - 1)
                                pb = ps0 if k % 2 == 0 else ps1
                                for dc in range(ND):
                                    t.matmul(pb[:], cfm[:, dc * T + tt * 128: dc * T + tt * 128 + 128],
                                             wp[:, dc * D + os * 512: dc * D + os * 512 + 512],
                                             start=(dc == 0), stop=False)
                                mmi = t.matmul(pb[:], ones1b[:], bpr[:, os * 512:(os + 1) * 512],
                                               start=False, stop=True)
                                pj.inc(mmi)

                    @b.vector
                    def _(v):
                        for tt in range(NT):
                            v.wait_ge(ld.sem, base_ld + (tt + 1) * 16)
                            for os in range(2):
                                k = tt * 2 + os
                                v.wait_ge(pj.sem, k + 1)
                                pb = ps0 if k % 2 == 0 else ps1
                                ev.inc(v.tensor_add(o1[:, os * 512:(os + 1) * 512],
                                                    x0t[:, os * 512:(os + 1) * 512], pb[:]))

                    @b.gpsimd
                    def _(gp):
                        for tt in range(NT):
                            gp.wait_ge(ev.sem, tt * 2 + 2)
                            dm.inc(gp.dma_start(out=d_x1p.ap()[tt * 128:(tt + 1) * 128, :], in_=o1[:]), 16)
                        gp.wait_ge(dm.sem, dm.v)
                nc.all_engine_barrier()

        if want("s4"):
            with contextlib.ExitStack() as st:
                wch = st.enter_context(nc.sbuf_tensor("s4_wch", [128, 2 * 4096], F32))
                wqo = st.enter_context(nc.sbuf_tensor("s4_wqo", [128, 2 * 4096], BF16))
                cs = st.enter_context(nc.sbuf_tensor("s4_cs", [128, 32], F32))
                t1 = st.enter_context(nc.sbuf_tensor("s4_t1", [32, 1], F32))
                t2 = st.enter_context(nc.sbuf_tensor("s4_t2", [1, 2], F32))
                swc = st.enter_context(nc.sbuf_tensor("s4_swc", [128, 2], F32))
                onescol = st.enter_context(nc.sbuf_tensor("s4_onc", [128, 1], F32))
                psW = st.enter_context(nc.psum_tensor("s4_psW", [128, 512], F32))
                ld = Ctr(st.enter_context(nc.semaphore("s4ld")))
                rd = Ctr(st.enter_context(nc.semaphore("s4rd")))
                qd = Ctr(st.enter_context(nc.semaphore("s4qd")))
                so = Ctr(st.enter_context(nc.semaphore("s4so")))
                fn = Ctr(st.enter_context(nc.semaphore("s4fn")))

                with nc.Block() as b:
                    @b.gpsimd
                    def _(gp):
                        gp.memset(onescol[:], 1.0)
                nc.all_engine_barrier()

                for wi, (wsrc, wdst, nch, ncols, nelem) in enumerate(
                        [(w_l0w1t, d_w1q, ND, 4096, D * 4096),
                         (w_l0w2t, d_w2q, 32, D, 4096 * D)]):
                    base_ld, base_rd = ld.v, rd.v
                    with nc.Block() as b:
                        @b.sync
                        def _(sync):
                            for c in range(nch):
                                if c >= 2:
                                    sync.wait_ge(rd.sem, base_rd + c - 1)
                                ld.inc(sync.dma_start(out=wch[:, (c % 2) * 4096:(c % 2) * 4096 + ncols],
                                                      in_=wsrc[c * 128:(c + 1) * 128, :]), 16)

                        @b.vector
                        def _(v):
                            for c in range(nch):
                                v.wait_ge(ld.sem, base_ld + (c + 1) * 16)
                                rd.inc(v.reduce_sum(cs[:, c:c + 1],
                                                    wch[:, (c % 2) * 4096:(c % 2) * 4096 + ncols],
                                                    AX.X, apply_absolute_value=True))
                    nc.all_engine_barrier()
                    with nc.Block() as b:
                        @b.tensor
                        def _(t):
                            t.matmul(psW[0:nch, 0:1], cs[:, 0:nch], onescol[:], start=True, stop=True)
                    nc.all_engine_barrier()
                    with nc.Block() as b:
                        @b.vector
                        def _(v):
                            v.tensor_copy(t1[0:nch, :], psW[0:nch, 0:1])
                    nc.all_engine_barrier()
                    with nc.Block() as b:
                        @b.tensor
                        def _(t):
                            t.matmul(psW[0:1, 0:1], t1[0:nch, :], onescol[0:nch, :], start=True, stop=True)
                    nc.all_engine_barrier()
                    with nc.Block() as b:
                        @b.vector
                        def _(v):
                            fn.inc(v.tensor_scalar(t2[0:1, 0:1], psW[0:1, 0:1], 1.0 / nelem, 1e-5,
                                                   ALU.mult, ALU.max))
                            fn.wait(v)
                            v.reciprocal(t2[0:1, 1:2], t2[0:1, 0:1])
                    nc.all_engine_barrier()
                    with nc.Block() as b:
                        @b.tensor
                        def _(t):
                            t.matmul(psW[:, 0:2], ones1f[:], t2[0:1, 0:2], start=True, stop=True)
                    nc.all_engine_barrier()
                    with nc.Block() as b:
                        @b.vector
                        def _(v):
                            v.tensor_copy(swc[:], psW[:, 0:2])
                    nc.all_engine_barrier()
                    with nc.Block() as b:
                        @b.sync
                        def _(sync):
                            ld.inc(sync.dma_start(out=d_rows2.ap()[wi:wi + 1, 0:1], in_=t2[0:1, 0:1],
                                                  allow_slow_non_contiguous=True), 16)
                    nc.all_engine_barrier()
                    base_ld, base_qd, base_so = ld.v, qd.v, so.v
                    with nc.Block() as b:
                        @b.sync
                        def _(sync):
                            for c in range(nch):
                                if c >= 2:
                                    sync.wait_ge(qd.sem, base_qd + (c - 1) * 3)
                                ld.inc(sync.dma_start(out=wch[:, (c % 2) * 4096:(c % 2) * 4096 + ncols],
                                                      in_=wsrc[c * 128:(c + 1) * 128, :]), 16)

                        @b.vector
                        def _(v):
                            for c in range(nch):
                                v.wait_ge(ld.sem, base_ld + (c + 1) * 16)
                                if c >= 2:
                                    v.wait_ge(so.sem, base_so + (c - 1) * 16)
                                sl = wch[:, (c % 2) * 4096:(c % 2) * 4096 + ncols]
                                ob = wqo[:, (c % 2) * 4096:(c % 2) * 4096 + ncols]
                                qd.inc(v.tensor_scalar(sl, sl, swc[:, 1:2], None, ALU.mult))
                                qd.wait(v)
                                qd.inc(v.tensor_scalar(sl, sl, MAGIC, -MAGIC, ALU.add, ALU.add))
                                qd.wait(v)
                                qd.inc(v.tensor_scalar(ob, sl, 1.0, -1.0, ALU.min, ALU.max))
                                qd.wait(v)

                        @b.gpsimd
                        def _(gp):
                            for c in range(nch):
                                gp.wait_ge(qd.sem, base_qd + (c + 1) * 3)
                                so.inc(gp.dma_start(out=wdst.ap()[c * 128:(c + 1) * 128, :],
                                                    in_=wqo[:, (c % 2) * 4096:(c % 2) * 4096 + ncols]), 16)
                            gp.wait_ge(so.sem, so.v)
                    nc.all_engine_barrier()

            with contextlib.ExitStack() as st:
                scB = st.enter_context(nc.sbuf_tensor("s4_scB", [128, T], F32))
                rcB = st.enter_context(nc.sbuf_tensor("s4_rcB", [128, T], F32))
                swb = st.enter_context(nc.sbuf_tensor("s4_swb", [128, 2], F32))
                swrow = st.enter_context(nc.sbuf_tensor("s4_swr", [1, 2], F32))
                hpad = st.enter_context(nc.sbuf_tensor("s4_hp", [128, 2050], F32))
                habs = st.enter_context(nc.sbuf_tensor("s4_habs", [128, T], F32))
                cva = st.enter_context(nc.sbuf_tensor("s4_cva", [128, T], F32))
                cvt = st.enter_context(nc.sbuf_tensor("s4_cvt", [128, T], F32))
                h2 = st.enter_context(nc.sbuf_tensor("s4_h2", [128, T], F32))
                am2 = st.enter_context(nc.sbuf_tensor("s4_am2", [128, T], F32))
                cwsb = st.enter_context(nc.sbuf_tensor("s4_cw", [128, 32 * 3], F32))
                cbsb = st.enter_context(nc.sbuf_tensor("s4_cb", [128, 32], F32))
                rcol2 = st.enter_context(nc.sbuf_tensor("s4_rc2", [128, NT], F32))
                art2 = st.enter_context(nc.sbuf_tensor("s4_art2", [16, 128], F32))
                ps0 = st.enter_context(nc.psum_tensor("s4_ps0", [128, 512], F32))
                ps1 = st.enter_context(nc.psum_tensor("s4_ps1", [128, 512], F32))
                ld = Ctr(st.enter_context(nc.semaphore("s4bld")))
                pj = Ctr(st.enter_context(nc.semaphore("s4bpj")))
                ev = Ctr(st.enter_context(nc.semaphore("s4bev")))
                cv = Ctr(st.enter_context(nc.semaphore("s4bcv")))
                hs = Ctr(st.enter_context(nc.semaphore("s4bhs")))
                fn = Ctr(st.enter_context(nc.semaphore("s4bfn")))

                with nc.Block() as b:
                    @b.sync
                    def _(sync):
                        ld.inc(sync.dma_start(out=swrow[:], in_=d_rows2.ap()[:, 0:1].rearrange("w o -> o w"),
                                              allow_slow_non_contiguous=True), 16)

                    @b.gpsimd
                    def _(gp):
                        ld.inc(gp.dma_start(out=cwsb[:].rearrange("p (c k) -> p c k", k=3),
                                            in_=w_l0cw.rearrange("(c p) k -> p c k", p=128)), 16)
                        ld.inc(gp.dma_start(out=cbsb[:], in_=w_l0cb[:, :]), 16)
                        gp.memset(hpad[:, 0:1], 0.0)
                        gp.memset(hpad[:, 2049:2050], 0.0)
                        ld.wait(gp)
                nc.all_engine_barrier()
                with nc.Block() as b:
                    @b.tensor
                    def _(t):
                        for ts2 in range(4):
                            pj.inc(t.matmul((ps0 if ts2 % 2 == 0 else ps1)[:, 0:512],
                                            ones1f[:], scale1_row[0:1, ts2 * 512:(ts2 + 1) * 512],
                                            start=True, stop=True))
                        for ts2 in range(4):
                            pj.inc(t.matmul((ps0 if ts2 % 2 == 0 else ps1)[:, 0:512],
                                            ones1f[:], recip1_row[0:1, ts2 * 512:(ts2 + 1) * 512],
                                            start=True, stop=True))
                        t.wait_ge(ev.sem, 8)
                        pj.inc(t.matmul(ps0[:, 0:2], ones1f[:], swrow[:], start=True, stop=True))

                    @b.vector
                    def _(v):
                        for ts2 in range(4):
                            v.wait_ge(pj.sem, ts2 + 1)
                            ev.inc(v.tensor_copy(scB[:, ts2 * 512:(ts2 + 1) * 512],
                                                 (ps0 if ts2 % 2 == 0 else ps1)[:, 0:512]))
                        for ts2 in range(4):
                            v.wait_ge(pj.sem, 4 + ts2 + 1)
                            ev.inc(v.tensor_copy(rcB[:, ts2 * 512:(ts2 + 1) * 512],
                                                 (ps0 if ts2 % 2 == 0 else ps1)[:, 0:512]))
                        v.wait_ge(pj.sem, 9)
                        fn.inc(v.tensor_copy(swb[:], ps0[:, 0:2]))
                        fn.wait(v)
                        for ts2 in range(4):
                            fn.inc(v.tensor_scalar(rcB[:, ts2 * 512:(ts2 + 1) * 512],
                                                   rcB[:, ts2 * 512:(ts2 + 1) * 512],
                                                   swb[:, 0:1], None, ALU.mult))
                        fn.wait(v)
                nc.all_engine_barrier()

                with contextlib.ExitStack() as s2:
                    xq1 = s2.enter_context(nc.sbuf_tensor("s4_xq1", [128, ND * T], BF16))
                    xch = s2.enter_context(nc.sbuf_tensor("s4_xch", [128, 2 * T], F32))
                    w1b = s2.enter_context(nc.sbuf_tensor("s4_w1b", [128, 2 * ND * 128], BF16))
                    q1 = Ctr(s2.enter_context(nc.semaphore("s4q1")))
                    sil = Ctr(s2.enter_context(nc.semaphore("s4sil")))
                    sab = Ctr(s2.enter_context(nc.semaphore("s4sab")))
                    amx2 = Ctr(s2.enter_context(nc.semaphore("s4am")))

                    base_ld = ld.v
                    with nc.Block() as b:
                        @b.sync
                        def _(sync):
                            for dc in range(ND):
                                if dc >= 2:
                                    sync.wait_ge(q1.sem, (dc - 1) * 2)
                                sync.dma_start(out=xch[:, (dc % 2) * T:(dc % 2) * T + T],
                                               in_=d_xn0fm.ap()[dc * 128:(dc + 1) * 128, :]).then_inc(ld.sem, 16)
                                ld.v += 16

                        @b.vector
                        def _(v):
                            for dc in range(ND):
                                v.wait_ge(ld.sem, base_ld + (dc + 1) * 16)
                                sl2 = xch[:, (dc % 2) * T:(dc % 2) * T + T]
                                q1.inc(v.tensor_mul(sl2, sl2, scB[:]))
                                q1.wait(v)
                                q1.inc(v.tensor_scalar(xq1[:, dc * T:(dc + 1) * T], sl2,
                                                       MAGIC, -MAGIC, ALU.add, ALU.add))
                                q1.wait(v)
                    nc.all_engine_barrier()

                    pjb, evb, cvb, hsb, ldb = pj.v, ev.v, cv.v, hs.v, ld.v
                    with nc.Block() as b:
                        @b.sync
                        def _(sync):
                            for oc in range(32):
                                if oc >= 2:
                                    sync.wait_ge(pj.sem, pjb + (oc - 1) * 4)
                                sync.dma_start(
                                    out=w1b[:, (oc % 2) * ND * 128:(oc % 2 + 1) * ND * 128].rearrange("p (c o) -> p c o", o=128),
                                    in_=d_w1q.ap().rearrange("(c p) o -> p c o", p=128)[:, :, oc * 128:(oc + 1) * 128]
                                ).then_inc(ld.sem, 16)
                                ld.v += 16

                        @b.tensor
                        def _(t):
                            for oc in range(32):
                                t.wait_ge(ld.sem, ldb + (oc + 1) * 16)
                                for ts2 in range(4):
                                    k = oc * 4 + ts2
                                    if k >= 2:
                                        t.wait_ge(ev.sem, evb + k - 1)
                                    pb = ps0 if k % 2 == 0 else ps1
                                    for dc in range(ND):
                                        mmi = t.matmul(pb[:, 0:512],
                                                       w1b[:, (oc % 2) * ND * 128 + dc * 128:(oc % 2) * ND * 128 + dc * 128 + 128],
                                                       xq1[:, dc * T + ts2 * 512: dc * T + ts2 * 512 + 512],
                                                       start=(dc == 0), stop=(dc == ND - 1))
                                    pj.inc(mmi)

                        @b.vector
                        def _(v):
                            for oc in range(32):
                                if oc >= 1:
                                    v.wait_ge(cv.sem, cvb + oc)
                                for ts2 in range(4):
                                    k = oc * 4 + ts2
                                    v.wait_ge(pj.sem, pjb + k + 1)
                                    pb = ps0 if k % 2 == 0 else ps1
                                    ev.inc(v.tensor_mul(hpad[:, 1 + ts2 * 512: 1 + (ts2 + 1) * 512],
                                                        pb[:, 0:512], rcB[:, ts2 * 512:(ts2 + 1) * 512]))
                                fn.inc(v.tensor_scalar(cva[:], hpad[:, 0:2048], cwsb[:, oc * 3: oc * 3 + 1], None, ALU.mult))
                                fn.inc(v.tensor_scalar(cvt[:], hpad[:, 1:2049], cwsb[:, oc * 3 + 1: oc * 3 + 2], None, ALU.mult))
                                fn.wait(v)
                                fn.inc(v.tensor_add(cva[:], cva[:], cvt[:]))
                                fn.wait(v)
                                fn.inc(v.tensor_scalar(cvt[:], hpad[:, 2:2050], cwsb[:, oc * 3 + 2: oc * 3 + 3], None, ALU.mult))
                                fn.wait(v)
                                cv.inc(v.tensor_add(cva[:], cva[:], cvt[:]))
                                v.wait_ge(sab.sem, oc + 1)
                                if oc == 0:
                                    amx2.inc(v.tensor_copy(am2[:], habs[:]))
                                else:
                                    amx2.inc(v.tensor_max(am2[:], am2[:], habs[:]))
                                amx2.wait(v)

                        @b.scalar
                        def _(s3):
                            for oc in range(32):
                                s3.wait_ge(cv.sem, cvb + oc + 1)
                                if oc >= 1:
                                    s3.wait_ge(hs.sem, hsb + oc * 16)
                                sil.inc(s3.activation(h2[:], cva[:], AF.Silu,
                                                      bias=cbsb[:, oc: oc + 1]))
                                sil.wait(s3)
                                if oc >= 1:
                                    s3.wait_ge(amx2.sem, oc)
                                sab.inc(s3.activation(habs[:], h2[:], AF.Abs))

                        @b.gpsimd
                        def _(gp):
                            for oc in range(32):
                                gp.wait_ge(sil.sem, oc + 1)
                                hs.inc(gp.dma_start(out=d_h.ap()[oc * 128:(oc + 1) * 128, :], in_=h2[:]), 16)
                            gp.wait_ge(hs.sem, hs.v)
                    nc.all_engine_barrier()

                pjb = pj.v
                with nc.Block() as b:
                    @b.tensor
                    def _(t):
                        for i in range(NT):
                            if i >= 2:
                                t.wait_ge(ev.sem, ev.v + i - 1)
                            pb = ps0 if i % 2 == 0 else ps1
                            pj.inc(t.transpose(pb[:, 0:128], am2[:, i * 128:(i + 1) * 128], ident[:]))

                    @b.vector
                    def _(v):
                        for i in range(NT):
                            v.wait_ge(pj.sem, pjb + i + 1)
                            pb = ps0 if i % 2 == 0 else ps1
                            ev.inc(v.reduce_max(rcol2[:, i:i + 1], pb[:, 0:128], AX.X))
                nc.all_engine_barrier()
                with nc.Block() as b:
                    @b.vector
                    def _(v):
                        fn.inc(v.tensor_scalar(rcol2[:], rcol2[:], 1e-5, None, ALU.max))
                        fn.wait(v)
                nc.all_engine_barrier()
                with nc.Block() as b:
                    @b.tensor
                    def _(t):
                        t.transpose(ps0[0:16, 0:128], rcol2[:], ident[:])
                nc.all_engine_barrier()
                with nc.Block() as b:
                    @b.vector
                    def _(v):
                        v.tensor_copy(art2[:], ps0[0:16, 0:128])
                nc.all_engine_barrier()
                with nc.Block() as b:
                    @b.sync
                    def _(sync):
                        ld.inc(sync.dma_start(out=d_rows2.ap()[1:2, :].rearrange("o (i p) -> (o i) p", p=128),
                                              in_=art2[:]), 16)
                        ld.wait(sync)
                        ld.inc(sync.dma_start(out=scale1_row[:], in_=d_rows2.ap()[1:2, :]), 16)

                    @b.vector
                    def _(v):
                        ld.wait(v)
                        fn.inc(v.reciprocal(recip1_row[:], scale1_row[:]))
                        fn.wait(v)
                        fn.inc(v.tensor_scalar(recip1_row[:], recip1_row[:], 127.0, None, ALU.mult))
                        fn.wait(v)
                        fn.inc(v.tensor_scalar(rcol2[:], rcol2[:], swb[:, 0:2][:, 1:2] if False else swb[:, 1:2], None, ALU.mult))
                        fn.wait(v)
                        fn.inc(v.tensor_scalar(rcol2[:], rcol2[:], 1.0 / 127.0, None, ALU.mult))
                        fn.wait(v)
                nc.all_engine_barrier()
                pjb = pj.v
                with nc.Block() as b:
                    @b.tensor
                    def _(t):
                        for ts2 in range(4):
                            pj.inc(t.matmul((ps0 if ts2 % 2 == 0 else ps1)[:, 0:512],
                                            ones1f[:], recip1_row[0:1, ts2 * 512:(ts2 + 1) * 512],
                                            start=True, stop=True))

                    @b.vector
                    def _(v):
                        for ts2 in range(4):
                            v.wait_ge(pj.sem, pjb + ts2 + 1)
                            ev.inc(v.tensor_copy(scB[:, ts2 * 512:(ts2 + 1) * 512],
                                                 (ps0 if ts2 % 2 == 0 else ps1)[:, 0:512]))
                nc.all_engine_barrier()

                with contextlib.ExitStack() as s2:
                    hch = s2.enter_context(nc.sbuf_tensor("s4_hch", [128, 2 * T], F32))
                    qch = s2.enter_context(nc.sbuf_tensor("s4_qch", [128, 2 * T], BF16))
                    q2 = Ctr(s2.enter_context(nc.semaphore("s4q2")))
                    so2 = Ctr(s2.enter_context(nc.semaphore("s4so2")))
                    base_ld = ld.v
                    with nc.Block() as b:
                        @b.sync
                        def _(sync):
                            for c in range(32):
                                if c >= 2:
                                    sync.wait_ge(q2.sem, (c - 1) * 2)
                                sync.dma_start(out=hch[:, (c % 2) * T:(c % 2) * T + T],
                                               in_=d_h.ap()[c * 128:(c + 1) * 128, :]).then_inc(ld.sem, 16)
                                ld.v += 16

                        @b.vector
                        def _(v):
                            for c in range(32):
                                v.wait_ge(ld.sem, base_ld + (c + 1) * 16)
                                if c >= 2:
                                    v.wait_ge(so2.sem, (c - 1) * 16)
                                sl2 = hch[:, (c % 2) * T:(c % 2) * T + T]
                                q2.inc(v.tensor_mul(sl2, sl2, scB[:]))
                                q2.wait(v)
                                q2.inc(v.tensor_scalar(qch[:, (c % 2) * T:(c % 2) * T + T], sl2,
                                                       MAGIC, -MAGIC, ALU.add, ALU.add))
                                q2.wait(v)

                        @b.gpsimd
                        def _(gp):
                            for c in range(32):
                                gp.wait_ge(q2.sem, (c + 1) * 2)
                                so2.inc(gp.dma_start(out=d_xq2.ap()[c * 128:(c + 1) * 128, :],
                                                     in_=qch[:, (c % 2) * T:(c % 2) * T + T]), 16)
                            gp.wait_ge(so2.sem, so2.v)
                    nc.all_engine_barrier()

                with contextlib.ExitStack() as s2:
                    w2b = s2.enter_context(nc.sbuf_tensor("s4_w2b", [128, 32 * D], BF16))
                    x2t = s2.enter_context(nc.sbuf_tensor("s4_x2t", [128, 2 * 32 * 128], BF16))
                    x1pt = s2.enter_context(nc.sbuf_tensor("s4_x1pt", [128, D], F32))
                    x1t = s2.enter_context(nc.sbuf_tensor("s4_x1t", [128, D], F32))
                    mml = Ctr(s2.enter_context(nc.semaphore("s4mml")))
                    with nc.Block() as b:
                        @b.gpsimd
                        def _(gp):
                            gp.dma_start(out=w2b[:].rearrange("p (c o) -> p c o", o=D),
                                         in_=d_w2q.ap().rearrange("(c p) o -> p c o", p=128)).then_inc(ld.sem, 16)
                            ld.v += 16
                            ld.wait(gp)
                    nc.all_engine_barrier()
                    base_ld = ld.v
                    evb2 = ev.v
                    pjb2 = pj.v
                    with nc.Block() as b:
                        @b.sync
                        def _(sync):
                            for tt in range(NT):
                                if tt >= 2:
                                    sync.wait_ge(pj.sem, pjb2 + (tt - 1) * 2)
                                sync.dma_start(
                                    out=x2t[:, (tt % 2) * 32 * 128:(tt % 2 + 1) * 32 * 128].rearrange("p (c t) -> p c t", t=128),
                                    in_=d_xq2.ap().rearrange("(c p) t -> p c t", p=128)[:, :, tt * 128:(tt + 1) * 128]
                                ).then_inc(ld.sem, 16)
                                ld.v += 16
                                if tt >= 1:
                                    sync.wait_ge(ev.sem, evb2 + tt * 2)
                                sync.dma_start(out=x1pt[:], in_=d_x1p.ap()[tt * 128:(tt + 1) * 128, :]).then_inc(ld.sem, 16)
                                ld.v += 16

                        @b.tensor
                        def _(t):
                            for tt in range(NT):
                                t.wait_ge(ld.sem, base_ld + tt * 32 + 16)
                                for os2 in range(2):
                                    k = tt * 2 + os2
                                    if k >= 2:
                                        t.wait_ge(ev.sem, evb2 + k - 1)
                                    pb = ps0 if k % 2 == 0 else ps1
                                    for dc in range(32):
                                        mmi = t.matmul(pb[:, 0:512],
                                                       x2t[:, (tt % 2) * 32 * 128 + dc * 128:(tt % 2) * 32 * 128 + dc * 128 + 128],
                                                       w2b[:, dc * D + os2 * 512: dc * D + os2 * 512 + 512],
                                                       start=(dc == 0), stop=(dc == 31))
                                    pj.inc(mmi)

                        @b.vector
                        def _(v):
                            for tt in range(NT):
                                v.wait_ge(ld.sem, base_ld + (tt + 1) * 32)
                                if tt >= 1:
                                    v.wait_ge(mml.sem, tt * 16)
                                for os2 in range(2):
                                    k = tt * 2 + os2
                                    v.wait_ge(pj.sem, pjb2 + k + 1)
                                    pb = ps0 if k % 2 == 0 else ps1
                                    fn.inc(v.tensor_scalar(x1t[:, os2 * 512:(os2 + 1) * 512], pb[:, 0:512],
                                                           rcol2[:, tt:tt + 1], None, ALU.mult))
                                    fn.wait(v)
                                    ev.inc(v.tensor_add(x1t[:, os2 * 512:(os2 + 1) * 512],
                                                        x1t[:, os2 * 512:(os2 + 1) * 512],
                                                        x1pt[:, os2 * 512:(os2 + 1) * 512]))

                        @b.gpsimd
                        def _(gp):
                            for tt in range(NT):
                                gp.wait_ge(ev.sem, evb2 + tt * 2 + 2)
                                mml.inc(gp.dma_start(out=d_x1.ap()[tt * 128:(tt + 1) * 128, :], in_=x1t[:]), 16)
                            gp.wait_ge(mml.sem, mml.v)
                    nc.all_engine_barrier()
        if want("s5"):
            with contextlib.ExitStack() as st:
                xsb = st.enter_context(nc.sbuf_tensor("s5_x", [128, NT * D], F32))
                wsb = st.enter_context(nc.sbuf_tensor("s5_w", [128, NT * D], F32))
                hib = st.enter_context(nc.sbuf_tensor("s5_hi", [128, NT * D], BF16))
                s1 = st.enter_context(nc.sbuf_tensor("s5_s1", [128, NT], F32))
                s2 = st.enter_context(nc.sbuf_tensor("s5_s2", [128, NT], F32))
                mm = st.enter_context(nc.sbuf_tensor("s5_m", [128, NT], F32))
                rs = st.enter_context(nc.sbuf_tensor("s5_rs", [128, NT], F32))
                ld = Ctr(st.enter_context(nc.semaphore("s5ld")))
                fn = Ctr(st.enter_context(nc.semaphore("s5fn")))

                with nc.Block() as b:
                    @b.sync
                    def _(sync):
                        ld.inc(sync.dma_start(out=xsb[:].rearrange("p (i d) -> p i d", d=D),
                                              in_=d_x1.ap().rearrange("(i p) d -> p i d", p=128)), 16)

                    @b.scalar
                    def _(s3):
                        ld.wait(s3)
                        for i in range(NT):
                            s3.activation(wsb[:, i * D:(i + 1) * D], xsb[:, i * D:(i + 1) * D], AF.Square)
                nc.all_engine_barrier()
                with nc.Block() as b:
                    @b.vector
                    def _(v):
                        v.reduce_sum(s1[:], xsb[:].rearrange("p (i d) -> p i d", d=D), AX.X)
                        v.reduce_sum(s2[:], wsb[:].rearrange("p (i d) -> p i d", d=D), AX.X)
                        v.tensor_scalar(mm[:], s1[:], 1.0 / D, None, ALU.mult)
                        v.tensor_scalar(rs[:], s2[:], 1.0 / D, None, ALU.mult)
                        fn.inc(v.tensor_tensor(s1[:], mm[:], mm[:], ALU.mult))
                        fn.wait(v)
                        v.tensor_sub(s2[:], rs[:], s1[:])
                nc.all_engine_barrier()
                with nc.Block() as b:
                    @b.scalar
                    def _(s3):
                        s3.activation(s2[:], s2[:], AF.Sqrt, bias=cst(EPS_LN))
                nc.all_engine_barrier()
                with nc.Block() as b:
                    @b.vector
                    def _(v):
                        fn.inc(v.reciprocal(rs[:], s2[:]))
                        fn.wait(v)
                        for i in range(NT):
                            v.tensor_scalar(wsb[:, i * D:(i + 1) * D],
                                            xsb[:, i * D:(i + 1) * D],
                                            mm[:, i:i + 1], rs[:, i:i + 1],
                                            ALU.subtract, ALU.mult)
                        fn.inc(v.tensor_copy(hib[:], wsb[:]))
                        fn.wait(v)
                        fn.inc(v.tensor_copy(xsb[:], hib[:]))
                        fn.wait(v)
                        fn.inc(v.tensor_sub(wsb[:], wsb[:], xsb[:]))
                        fn.wait(v)
                nc.all_engine_barrier()
                with nc.Block() as b:
                    @b.gpsimd
                    def _(gp):
                        gp.dma_start(out=d_xn1tm.ap().rearrange("(i p) d -> p i d", p=128),
                                     in_=hib[:].rearrange("p (i d) -> p i d", d=D)).then_inc(ld.sem, 16)
                        ld.v += 16
                        ld.wait(gp)
                nc.all_engine_barrier()
                with nc.Block() as b:
                    @b.vector
                    def _(v):
                        v.tensor_copy(hib[:], wsb[:])
                nc.all_engine_barrier()
                with nc.Block() as b:
                    @b.gpsimd
                    def _(gp):
                        gp.dma_start(out=d_xn1lo.ap().rearrange("(i p) d -> p i d", p=128),
                                     in_=hib[:].rearrange("p (i d) -> p i d", d=D)).then_inc(ld.sem, 16)
                        ld.v += 16
                        ld.wait(gp)
                nc.all_engine_barrier()

        if want("s8"):
            with contextlib.ExitStack() as st:
                xnf = st.enter_context(nc.sbuf_tensor("s8_xnf", [128, ND * T], BF16))
                wbuf = st.enter_context(nc.sbuf_tensor("s8_wbuf", [128, ND * D], BF16))
                bqc = st.enter_context(nc.sbuf_tensor("s8_bqc", [128, ND], F32))
                bkc = st.enter_context(nc.sbuf_tensor("s8_bkc", [128, ND], F32))
                bvr = st.enter_context(nc.sbuf_tensor("s8_bvr", [1, D], BF16))
                stg = st.enter_context(nc.sbuf_tensor("s8_stg", [128, 2 * 512], BF16))
                negMc = st.enter_context(nc.sbuf_tensor("s8_negMc", [128, 1], F32))
                ps0 = st.enter_context(nc.psum_tensor("s8_ps0", [128, 512], F32))
                ps1 = st.enter_context(nc.psum_tensor("s8_ps1", [128, 512], F32))
                ld = Ctr(st.enter_context(nc.semaphore("s8ld")))
                pj = Ctr(st.enter_context(nc.semaphore("s8pj")))
                ev = Ctr(st.enter_context(nc.semaphore("s8ev")))
                so = Ctr(st.enter_context(nc.semaphore("s8so")))
                fn = Ctr(st.enter_context(nc.semaphore("s8fn")))

                with nc.Block() as b:
                    @b.sync
                    def _(sync):
                        for c in range(ND):
                            ld.inc(sync.dma_start_transpose(xnf[:, c * T:(c + 1) * T],
                                                            d_xn1tm.ap()[:, c * 128:(c + 1) * 128]), 16)

                    @b.gpsimd
                    def _(gp):
                        ld.inc(gp.dma_start(out=bqc[:], in_=w_l1bq[:, :]), 16)
                        ld.inc(gp.dma_start(out=bkc[:], in_=w_l1bk[:, :]), 16)
                        ld.inc(gp.dma_start(out=bvr[:], in_=w_l1bv[:, :]), 16)
                        ld.wait(gp)
                nc.all_engine_barrier()

                for proj in range(3):
                    wsrc = (w_l1wqt, w_l1wkt, w_l1wvt)[proj]
                    with nc.Block() as b:
                        @b.gpsimd
                        def _(gp):
                            ld.inc(gp.dma_start(out=wbuf[:].rearrange("p (c o) -> p c o", o=D),
                                                in_=wsrc.rearrange("(c p) o -> p c o", p=128)), 16)
                            ld.wait(gp)
                    nc.all_engine_barrier()
                    pjb, evb, sob = pj.v, ev.v, so.v
                    if proj < 2:
                        bc = bqc if proj == 0 else bkc
                        dstfm = d_q1fm if proj == 0 else d_k1fm
                        with nc.Block() as b:
                            @b.tensor
                            def _(t):
                                k = 0
                                for oc in range(ND):
                                    for ts2 in range(4):
                                        if k >= 2:
                                            t.wait_ge(ev.sem, evb + k - 1)
                                        pb = ps0 if k % 2 == 0 else ps1
                                        for dc in range(ND):
                                            mmi = t.matmul(pb[:, 0:512],
                                                           wbuf[:, dc * D + oc * 128: dc * D + oc * 128 + 128],
                                                           xnf[:, dc * T + ts2 * 512: dc * T + ts2 * 512 + 512],
                                                           start=(dc == 0), stop=(dc == ND - 1))
                                        pj.inc(mmi)
                                        k += 1

                            @b.scalar
                            def _(s3):
                                k = 0
                                for oc in range(ND):
                                    for ts2 in range(4):
                                        s3.wait_ge(pj.sem, pjb + k + 1)
                                        if k >= 2:
                                            s3.wait_ge(so.sem, sob + (k - 1) * 16)
                                        pb = ps0 if k % 2 == 0 else ps1
                                        ev.inc(s3.activation(stg[:, (k % 2) * 512:(k % 2) * 512 + 512],
                                                             pb[:, 0:512], AF.Identity, bias=bc[:, oc:oc + 1]))
                                        k += 1

                            @b.gpsimd
                            def _(gp):
                                k = 0
                                for oc in range(ND):
                                    for ts2 in range(4):
                                        gp.wait_ge(ev.sem, evb + k + 1)
                                        so.inc(gp.dma_start(
                                            out=dstfm.ap()[oc * 128:(oc + 1) * 128, ts2 * 512:(ts2 + 1) * 512],
                                            in_=stg[:, (k % 2) * 512:(k % 2) * 512 + 512]), 16)
                                        k += 1
                                gp.wait_ge(so.sem, so.v)
                        nc.all_engine_barrier()
                    else:
                        with nc.Block() as b:
                            @b.tensor
                            def _(t):
                                k = 0
                                for tt in range(NT):
                                    for os2 in range(2):
                                        if k >= 2:
                                            t.wait_ge(ev.sem, evb + k - 1)
                                        pb = ps0 if k % 2 == 0 else ps1
                                        for dc in range(ND):
                                            t.matmul(pb[:, 0:512],
                                                     xnf[:, dc * T + tt * 128: dc * T + tt * 128 + 128],
                                                     wbuf[:, dc * D + os2 * 512: dc * D + os2 * 512 + 512],
                                                     start=(dc == 0), stop=False)
                                        mmi = t.matmul(pb[:, 0:512], ones1b[:],
                                                       bvr[:, os2 * 512:(os2 + 1) * 512],
                                                       start=False, stop=True)
                                        pj.inc(mmi)
                                        k += 1

                            @b.vector
                            def _(v):
                                k = 0
                                for tt in range(NT):
                                    for os2 in range(2):
                                        v.wait_ge(pj.sem, pjb + k + 1)
                                        if k >= 2:
                                            v.wait_ge(so.sem, sob + (k - 1) * 16)
                                        pb = ps0 if k % 2 == 0 else ps1
                                        ev.inc(v.tensor_copy(stg[:, (k % 2) * 512:(k % 2) * 512 + 512],
                                                             pb[:, 0:512]))
                                        k += 1

                            @b.gpsimd
                            def _(gp):
                                k = 0
                                for tt in range(NT):
                                    for os2 in range(2):
                                        gp.wait_ge(ev.sem, evb + k + 1)
                                        so.inc(gp.dma_start(
                                            out=d_v1tm.ap()[tt * 128:(tt + 1) * 128, os2 * 512:(os2 + 1) * 512],
                                            in_=stg[:, (k % 2) * 512:(k % 2) * 512 + 512]), 16)
                                        k += 1
                                gp.wait_ge(so.sem, so.v)
                        nc.all_engine_barrier()

                with contextlib.ExitStack() as s2:
                    qtt = s2.enter_context(nc.sbuf_tensor("s8_qtt", [128, 2 * D], BF16))
                    for pair in range(2):
                        src_fm = d_q1fm if pair == 0 else d_k1fm
                        dst_tm = d_q1tm if pair == 0 else d_k1tm
                        base_ld, base_so = ld.v, so.v
                        with nc.Block() as b:
                            @b.sync
                            def _(sync):
                                for i in range(NT):
                                    if i >= 2:
                                        sync.wait_ge(so.sem, base_so + (i - 1) * 16)
                                    sync.dma_start_transpose(qtt[:, (i % 2) * D:(i % 2) * D + D],
                                                             src_fm.ap()[:, i * 128:(i + 1) * 128]).then_inc(ld.sem, 16)
                                    ld.v += 16

                            @b.gpsimd
                            def _(gp):
                                for i in range(NT):
                                    gp.wait_ge(ld.sem, base_ld + (i + 1) * 16)
                                    so.inc(gp.dma_start(out=dst_tm.ap()[i * 128:(i + 1) * 128, :],
                                                        in_=qtt[:, (i % 2) * D:(i % 2) * D + D]), 16)
                                gp.wait_ge(so.sem, so.v)
                        nc.all_engine_barrier()

                with contextlib.ExitStack() as s2:
                    o64 = s2.enter_context(nc.sbuf_tensor("s8_o64", [128, 2], BF16))
                    sq8 = s2.enter_context(nc.sbuf_tensor("s8_sq8", [128, 2 * T], BF16))
                    qch8 = s2.enter_context(nc.sbuf_tensor("s8_qch8", [128, 2 * T], BF16))
                    nr8 = s2.enter_context(nc.sbuf_tensor("s8_nr8", [2, 68], F32))
                    tnr = s2.enter_context(nc.sbuf_tensor("s8_tnr", [1, 8], F32))
                    sqd = Ctr(s2.enter_context(nc.semaphore("s8sqd")))
                    base_ld, pjb, evb = ld.v, pj.v, ev.v
                    with nc.Block() as b:
                        @b.gpsimd
                        def _(gp):
                            gp.dma_start(out=o64[:], in_=c_ones64[:, :]).then_inc(ld.sem, 16)
                            ld.v += 16

                        @b.sync
                        def _(sync):
                            kc = 0
                            for t8 in range(2):
                                srcd = d_q1fm if t8 == 0 else d_k1fm
                                for c in range(ND):
                                    if kc >= 2:
                                        sync.wait_ge(sqd.sem, kc - 1)
                                    sync.dma_start(out=qch8[:, (kc % 2) * T:(kc % 2) * T + T],
                                                   in_=srcd.ap()[c * 128:(c + 1) * 128, :]).then_inc(ld.sem, 16)
                                    ld.v += 16
                                    kc += 1

                        @b.scalar
                        def _(s3):
                            for kc in range(16):
                                s3.wait_ge(ld.sem, base_ld + 16 + (kc + 1) * 16)
                                if kc >= 2:
                                    s3.wait_ge(pj.sem, pjb + (kc - 1) * 4)
                                sqd.inc(s3.activation(sq8[:, (kc % 2) * T:(kc % 2) * T + T],
                                                      qch8[:, (kc % 2) * T:(kc % 2) * T + T], AF.Square))

                        @b.tensor
                        def _(t):
                            for kc in range(16):
                                t.wait_ge(sqd.sem, kc + 1)
                                for ts2 in range(4):
                                    kk = kc * 4 + ts2
                                    if kk >= 2:
                                        t.wait_ge(ev.sem, evb + kk - 1)
                                    pb = ps0 if kk % 2 == 0 else ps1
                                    pj.inc(t.matmul(pb[0:2, 0:512], o64[:],
                                                    sq8[:, (kc % 2) * T + ts2 * 512:(kc % 2) * T + ts2 * 512 + 512],
                                                    start=True, stop=True))

                        @b.vector
                        def _(v):
                            for kc in range(16):
                                for ts2 in range(4):
                                    kk = kc * 4 + ts2
                                    v.wait_ge(pj.sem, pjb + kk + 1)
                                    pb = ps0 if kk % 2 == 0 else ps1
                                    ev.inc(v.reduce_max(nr8[0:2, 2 + kk:3 + kk], pb[0:2, 0:512], AX.X))
                    nc.all_engine_barrier()
                    with nc.Block() as b:
                        @b.vector
                        def _(v):
                            fn.inc(v.reduce_max(nr8[0:2, 0:1], nr8[0:2, 2:34], AX.X))
                            fn.inc(v.reduce_max(nr8[0:2, 1:2], nr8[0:2, 34:66], AX.X))
                            fn.wait(v)
                    nc.all_engine_barrier()
                    with nc.Block() as b:
                        @b.sync
                        def _(sync):
                            ld.inc(sync.dma_start(out=d_csrow.ap()[5:6, :], in_=nr8[0:2, 0:2],
                                                  allow_slow_non_contiguous=True), 16)
                            ld.wait(sync)
                            ld.inc(sync.dma_start(out=tnr[0:1, 0:4], in_=d_csrow.ap()[5:6, :]), 16)

                        @b.vector
                        def _(v):
                            ld.wait(v)
                            fn.inc(v.reduce_max(tnr[0:1, 4:5], tnr[0:1, 0:2], AX.X))
                            fn.inc(v.reduce_max(tnr[0:1, 5:6], tnr[0:1, 2:4], AX.X))
                            fn.wait(v)
                            fn.inc(v.tensor_tensor(tnr[0:1, 6:7], tnr[0:1, 4:5], tnr[0:1, 5:6], ALU.mult))
                            fn.wait(v)
                    nc.all_engine_barrier()
                    with nc.Block() as b:
                        @b.scalar
                        def _(s3):
                            s3.activation(tnr[0:1, 7:8], tnr[0:1, 6:7], AF.Sqrt)
                    nc.all_engine_barrier()
                    with nc.Block() as b:
                        @b.vector
                        def _(v):
                            fn.inc(v.tensor_scalar(tnr[0:1, 7:8], tnr[0:1, 7:8], -1.02 / 8.0, None, ALU.mult))
                            fn.wait(v)
                    nc.all_engine_barrier()
                    with nc.Block() as b:
                        @b.tensor
                        def _(t):
                            t.matmul(ps0[:, 0:1], ones1f[:], tnr[0:1, 7:8], start=True, stop=True)
                    nc.all_engine_barrier()
                    with nc.Block() as b:
                        @b.vector
                        def _(v):
                            v.tensor_copy(negMc[:], ps0[:, 0:1])
                nc.all_engine_barrier()

                def fixed_blocks(qb):
                    return sorted(set(x for x in (qb - 1, qb, qb + 1) if 0 <= x < 32) | {0, 31})

                with contextlib.ExitStack() as s2:
                    qh = s2.enter_context(nc.sbuf_tensor("s8_qh", [64, 2 * T], BF16))
                    kh = s2.enter_context(nc.sbuf_tensor("s8_kh", [64, 2 * T], BF16))
                    qsl = s2.enter_context(nc.sbuf_tensor("s8_qsl", [128, 2 * 1024], BF16))
                    ksl = s2.enter_context(nc.sbuf_tensor("s8_ksl", [128, 2 * 1024], BF16))
                    psT0 = s2.enter_context(nc.psum_tensor("s8_psT0", [128, 1024], BF16))
                    psT1 = s2.enter_context(nc.psum_tensor("s8_psT1", [128, 1024], BF16))
                    identb = s2.enter_context(nc.sbuf_tensor("s8_identb", [128, 128], BF16))
                    tp8 = Ctr(s2.enter_context(nc.semaphore("s8tp")))
                    te8 = Ctr(s2.enter_context(nc.semaphore("s8te")))
                    vtl = s2.enter_context(nc.sbuf_tensor("s8_vtl", [64, 2 * 32 * 65], BF16))
                    exp8 = s2.enter_context(nc.sbuf_tensor("s8_exp8", [64, 2 * 384], BF16))
                    accb = s2.enter_context(nc.sbuf_tensor("s8_accb", [65, T], F32))
                    psC0 = s2.enter_context(nc.psum_tensor("s8_psC0", [128, 512], F32))
                    psC1 = s2.enter_context(nc.psum_tensor("s8_psC1", [128, 512], F32))
                    hl = Ctr(s2.enter_context(nc.semaphore("s8hl")))
                    mm8 = Ctr(s2.enter_context(nc.semaphore("s8mm")))
                    ex8 = Ctr(s2.enter_context(nc.semaphore("s8ex")))
                    cx8 = Ctr(s2.enter_context(nc.semaphore("s8cx")))
                    ev8 = Ctr(s2.enter_context(nc.semaphore("s8evD")))
                    st8 = Ctr(s2.enter_context(nc.semaphore("s8st")))

                    vview = d_v1tm.ap().rearrange("t (m dd) -> (t m) dd", dd=64)

                    with nc.Block() as b:
                        @b.gpsimd
                        def _(gp):
                            gp.memset(vtl[:].rearrange("p (s c) -> p s c", c=65)[:, :, 64:65], 1.0)

                        @b.vector
                        def _(v):
                            v.tensor_copy(identb[:], ident[:])
                    nc.all_engine_barrier()

                    ng = {}
                    cnt = 0
                    for h in range(L1_H):
                        for qb in range(32):
                            cnt += 1
                            ng[(h, qb)] = cnt

                    with nc.Block() as b:
                        @b.sync
                        def _(sync):
                            for h in range(L1_H):
                                par = h % 2
                                if h >= 2:
                                    sync.wait_ge(cx8.sem, ng[(h - 1, 31)])
                                sync.dma_start(
                                    out=qsl[:, par * 1024:(par + 1) * 1024].rearrange("p (c d) -> p c d", d=64),
                                    in_=d_q1tm.ap()[h * 128:(h + 1) * 128, :].rearrange("t (m d) -> (t m) d", d=64)
                                        .rearrange("(c p) d -> p c d", p=128)
                                ).then_inc(hl.sem, 16)
                                sync.dma_start(
                                    out=ksl[:, par * 1024:(par + 1) * 1024].rearrange("p (c d) -> p c d", d=64),
                                    in_=d_k1tm.ap()[h * 128:(h + 1) * 128, :].rearrange("t (m d) -> (t m) d", d=64)
                                        .rearrange("(c p) d -> p c d", p=128)
                                ).then_inc(hl.sem, 16)
                                sync.dma_start(
                                    out=vtl[:, par * 2080:(par + 1) * 2080].rearrange("p (s c) -> p s c", c=65)[:, :, 0:64],
                                    in_=d_v1tm.ap()[h * 128:(h + 1) * 128, :].rearrange("t (m dd) -> (t m) dd", dd=64).rearrange("(s p) dd -> p s dd", p=64)
                                ).then_inc(hl.sem, 16)
                                hl.v += 48

                        @b.tensor
                        def _(t):
                            for h in range(L1_H):
                                par = h % 2
                                t.wait_ge(hl.sem, (h + 1) * 48)
                                if h >= 2:
                                    t.wait_ge(mm8.sem, ng[(h - 1, 31)])
                                for sl, _buf in ((qsl, qh), (ksl, kh)):
                                    for c in range(16):
                                        kk2 = tp8.v
                                        if kk2 >= 2:
                                            t.wait_ge(te8.sem, kk2 - 1)
                                        pt = psT0 if kk2 % 2 == 0 else psT1
                                        tp8.inc(t.transpose(pt[0:64, 0:128],
                                                            sl[:, par * 1024 + c * 64: par * 1024 + c * 64 + 64],
                                                            identb[:]))
                                t.wait_ge(te8.sem, 32 * (h + 1))
                                for qb in range(32):
                                    S = fixed_blocks(qb)
                                    g = ng[(h, qb)]
                                    pb = ps0 if g % 2 == 0 else ps1
                                    if g > 2:
                                        t.wait_ge(ex8.sem, g - 2)
                                    for si, kb in enumerate(S):
                                        mmi = t.matmul(
                                            pb[0:64, si * 64:(si + 1) * 64],
                                            kh[:, par * T + kb * 64: par * T + kb * 64 + 64],
                                            qh[:, par * T + qb * 64: par * T + qb * 64 + 64],
                                            start=(si == 0), stop=(si == len(S) - 1))
                                    mm8.inc(mmi)
                                    t.wait_ge(ex8.sem, g)
                                    pc = psC0 if g % 2 == 0 else psC1
                                    if g > 2:
                                        t.wait_ge(ev8.sem, g - 2)
                                    for si, kb in enumerate(S):
                                        mmi = t.matmul(
                                            pc[0:65, 0:64],
                                            vtl[:, par * 2080 + kb * 65: par * 2080 + kb * 65 + 65],
                                            exp8[:, (g % 2) * 384 + si * 64:(g % 2) * 384 + si * 64 + 64],
                                            start=(si == 0), stop=(si == len(S) - 1))
                                    cx8.inc(mmi)

                        @b.scalar
                        def _(s3):
                            for h in range(L1_H):
                                par = h % 2
                                for sli, (sl, buf) in enumerate(((qsl, qh), (ksl, kh))):
                                    for c in range(16):
                                        kk2 = te8.v
                                        s3.wait_ge(tp8.sem, kk2 + 1)
                                        pt = psT0 if kk2 % 2 == 0 else psT1
                                        te8.inc(s3.activation(buf[:, par * T + c * 128: par * T + c * 128 + 128],
                                                              pt[0:64, 0:128], AF.Copy))
                                for qb in range(32):
                                    S = fixed_blocks(qb)
                                    g = ng[(h, qb)]
                                    s3.wait_ge(mm8.sem, g)
                                    if g > 2:
                                        s3.wait_ge(cx8.sem, g - 2)
                                    pb = ps0 if g % 2 == 0 else ps1
                                    ex8.inc(s3.activation(exp8[:, (g % 2) * 384:(g % 2) * 384 + len(S) * 64],
                                                          pb[0:64, 0:len(S) * 64], AF.Exp,
                                                          bias=negMc[0:64, 0:1], scale=0.125))

                        @b.vector
                        def _(v):
                            for h in range(L1_H):
                                for qb in range(32):
                                    g = ng[(h, qb)]
                                    v.wait_ge(cx8.sem, g)
                                    if qb == 0 and h >= 1:
                                        v.wait_ge(st8.sem, h * 16)
                                    pc = psC0 if g % 2 == 0 else psC1
                                    ev8.inc(v.tensor_copy(accb[:, qb * 64:(qb + 1) * 64], pc[0:65, 0:64]))

                        @b.gpsimd
                        def _(gp):
                            for h in range(L1_H):
                                gp.wait_ge(ev8.sem, ng[(h, 31)])
                                st8.inc(gp.dma_start(out=d_l1acc.ap()[h * 65:(h + 1) * 65, :], in_=accb[:]), 16)
                            gp.wait_ge(st8.sem, st8.v)
                    nc.all_engine_barrier()

                with contextlib.ExitStack() as s2:
                    srcb = s2.enter_context(nc.sbuf_tensor("s8e_src", [1, 4], I32))
                    tgtb = s2.enter_context(nc.sbuf_tensor("s8e_tgt", [1, 4], I32))
                    srcf = s2.enter_context(nc.sbuf_tensor("s8e_srcf", [1, 4], F32))
                    tgtf = s2.enter_context(nc.sbuf_tensor("s8e_tgtf", [1, 4], F32))
                    dtmp = s2.enter_context(nc.sbuf_tensor("s8e_dtmp", [1, 16], F32))
                    delt = s2.enter_context(nc.sbuf_tensor("s8e_delt", [1, 4], F32))
                    dcol = s2.enter_context(nc.sbuf_tensor("s8e_dcol", [128, 4], F32))
                    iota64 = s2.enter_context(nc.sbuf_tensor("s8e_io64", [64, 1], F32))
                    iota65 = s2.enter_context(nc.sbuf_tensor("s8e_io65", [65, 1], F32))
                    sc64 = s2.enter_context(nc.sbuf_tensor("s8e_sc64", [128, 12], F32))
                    idxQf = s2.enter_context(nc.sbuf_tensor("s8e_idxQf", [64, 48], F32))
                    idxKf = s2.enter_context(nc.sbuf_tensor("s8e_idxKf", [64, 48], F32))
                    idxAf = s2.enter_context(nc.sbuf_tensor("s8e_idxAf", [65, 48], F32))
                    idxQ = s2.enter_context(nc.sbuf_tensor("s8e_idxQ", [64, 48], I32))
                    idxK = s2.enter_context(nc.sbuf_tensor("s8e_idxK", [64, 48], I32))
                    idxA = s2.enter_context(nc.sbuf_tensor("s8e_idxA", [65, 48], I32))
                    qblk = s2.enter_context(nc.sbuf_tensor("s8e_qblk", [64, 2 * 64], BF16))
                    kblk = s2.enter_context(nc.sbuf_tensor("s8e_kblk", [64, 2 * 64], BF16))
                    vblk = s2.enter_context(nc.sbuf_tensor("s8e_vblk", [64, 2 * 65], BF16))
                    qfmE = s2.enter_context(nc.sbuf_tensor("s8e_qfm", [64, 2 * 64], BF16))
                    kfmE = s2.enter_context(nc.sbuf_tensor("s8e_kfm", [64, 2 * 64], BF16))
                    expE = s2.enter_context(nc.sbuf_tensor("s8e_exp", [64, 2 * 64], BF16))
                    extE = s2.enter_context(nc.sbuf_tensor("s8e_ext", [65, 2 * 64], F32))
                    psE = s2.enter_context(nc.psum_tensor("s8e_psE", [128, 512], F32))
                    psE2 = s2.enter_context(nc.psum_tensor("s8e_psE2", [128, 512], F32))
                    psTq = s2.enter_context(nc.psum_tensor("s8e_psTq", [128, 1024], BF16))
                    psTk = s2.enter_context(nc.psum_tensor("s8e_psTk", [128, 1024], BF16))
                    gl = Ctr(s2.enter_context(nc.semaphore("s8egl")))
                    tpE = Ctr(s2.enter_context(nc.semaphore("s8etp")))
                    teE = Ctr(s2.enter_context(nc.semaphore("s8ete")))
                    mmE = Ctr(s2.enter_context(nc.semaphore("s8emm")))
                    exE = Ctr(s2.enter_context(nc.semaphore("s8eex")))
                    dmE = Ctr(s2.enter_context(nc.semaphore("s8edm")))
                    cxE = Ctr(s2.enter_context(nc.semaphore("s8ecx")))
                    evE = Ctr(s2.enter_context(nc.semaphore("s8eev")))
                    scE = Ctr(s2.enter_context(nc.semaphore("s8esc")))
                    fnE = Ctr(s2.enter_context(nc.semaphore("s8efn")))

                    with nc.Block() as b:
                        @b.gpsimd
                        def _(gp):
                            gl.inc(gp.dma_start(out=srcb[:], in_=w_l1src[:, :]), 16)
                            gl.inc(gp.dma_start(out=tgtb[:], in_=w_l1tgt[:, :]), 16)
                            gp.iota(iota64[:], [[0, 1]], channel_multiplier=1,
                                    allow_small_or_imprecise_dtypes=True)
                            gp.iota(iota65[:], [[0, 1]], channel_multiplier=32,
                                    allow_small_or_imprecise_dtypes=True)
                            gp.memset(vblk[:].rearrange("p (s c) -> p s c", c=65)[:, :, 64:65], 1.0)

                        @b.vector
                        def _(v):
                            gl.wait(v)
                            fnE.inc(v.tensor_copy(srcf[:], srcb[:]))
                            fnE.inc(v.tensor_copy(tgtf[:], tgtb[:]))
                            fnE.wait(v)
                            fnE.inc(v.tensor_sub(dtmp[0:1, 0:4], srcf[:], tgtf[:]))
                            fnE.wait(v)
                            fnE.inc(v.tensor_scalar(dtmp[0:1, 4:8], dtmp[0:1, 0:4], -1.0, None, ALU.is_ge))
                            fnE.inc(v.tensor_scalar(dtmp[0:1, 8:12], dtmp[0:1, 0:4], 1.0, None, ALU.is_le))
                            fnE.wait(v)
                            fnE.inc(v.tensor_tensor(dtmp[0:1, 0:4], dtmp[0:1, 4:8], dtmp[0:1, 8:12], ALU.mult))
                            fnE.inc(v.tensor_scalar(dtmp[0:1, 4:8], tgtf[:], 0.0, None, ALU.is_equal))
                            fnE.inc(v.tensor_scalar(dtmp[0:1, 8:12], tgtf[:], 31.0, None, ALU.is_equal))
                            fnE.wait(v)
                            fnE.inc(v.tensor_scalar(dtmp[0:1, 12:16], srcf[:], 0.0, None, ALU.mult))
                            fnE.wait(v)
                            for pp2, qq in (((1,), (0,)), ((2,), (0,)), ((2,), (1,))):
                                p2, q2 = pp2[0], qq[0]
                                fnE.inc(v.tensor_tensor(dtmp[0:1, 14 + 0:15 + 0] if False else dtmp[0:1, 6:7],
                                                        srcf[0:1, p2:p2 + 1], srcf[0:1, q2:q2 + 1], ALU.is_equal))
                                fnE.inc(v.tensor_tensor(dtmp[0:1, 7:8],
                                                        tgtf[0:1, p2:p2 + 1], tgtf[0:1, q2:q2 + 1], ALU.is_equal))
                                fnE.wait(v)
                                fnE.inc(v.tensor_tensor(dtmp[0:1, 6:7], dtmp[0:1, 6:7], dtmp[0:1, 7:8], ALU.mult))
                                fnE.wait(v)
                                fnE.inc(v.tensor_tensor(dtmp[0:1, 12 + p2:13 + p2],
                                                        dtmp[0:1, 12 + p2:13 + p2], dtmp[0:1, 6:7], ALU.max))
                                fnE.wait(v)
                            fnE.inc(v.tensor_scalar(delt[:], dtmp[0:1, 0:4], -1.0, 1.0, ALU.mult, ALU.add))
                            fnE.wait(v)
                            fnE.inc(v.tensor_scalar(dtmp[0:1, 4:8], dtmp[0:1, 4:8], -1.0, 1.0, ALU.mult, ALU.add))
                            fnE.inc(v.tensor_scalar(dtmp[0:1, 8:12], dtmp[0:1, 8:12], -1.0, 1.0, ALU.mult, ALU.add))
                            fnE.inc(v.tensor_scalar(dtmp[0:1, 12:16], dtmp[0:1, 12:16], -1.0, 1.0, ALU.mult, ALU.add))
                            fnE.wait(v)
                            fnE.inc(v.tensor_tensor(delt[:], delt[:], dtmp[0:1, 4:8], ALU.mult))
                            fnE.wait(v)
                            fnE.inc(v.tensor_tensor(delt[:], delt[:], dtmp[0:1, 8:12], ALU.mult))
                            fnE.wait(v)
                            fnE.inc(v.tensor_tensor(delt[:], delt[:], dtmp[0:1, 12:16], ALU.mult))
                            fnE.wait(v)
                    nc.all_engine_barrier()
                    with nc.Block() as b:
                        @b.tensor
                        def _(t):
                            t.matmul(psE[:, 0:4], ones1f[:], delt[:], start=True, stop=True)
                            t.matmul(psE[:, 4:8], ones1f[:], srcf[:], start=False, stop=True)
                            t.matmul(psE[:, 8:12], ones1f[:], tgtf[:], start=False, stop=True)
                    nc.all_engine_barrier()
                    with nc.Block() as b:
                        @b.vector
                        def _(v):
                            fnE.inc(v.tensor_copy(dcol[:], psE[:, 0:4]))
                            fnE.inc(v.tensor_copy(sc64[:, 8:12], psE[:, 4:8]))
                            fnE.wait(v)
                            fnE.inc(v.tensor_scalar(sc64[:, 0:4], psE[:, 4:8], 64.0, None, ALU.mult))
                            fnE.inc(v.tensor_scalar(sc64[:, 4:8], psE[:, 8:12], 64.0, None, ALU.mult))
                            fnE.wait(v)
                    nc.all_engine_barrier()
                    with nc.Block() as b:
                        @b.vector
                        def _(v):
                            for p2 in range(3):
                                for h in range(L1_H):
                                    col = p2 * 16 + h
                                    fnE.inc(v.tensor_scalar(idxQf[:, col:col + 1], iota64[:],
                                                            sc64[0:64, p2:p2 + 1], float(h * 2048) + 0.0,
                                                            ALU.add, ALU.add))
                                    fnE.inc(v.tensor_scalar(idxKf[:, col:col + 1], iota64[:],
                                                            sc64[0:64, 4 + p2:5 + p2], float(h * 2048) + 0.0,
                                                            ALU.add, ALU.add))
                                    fnE.inc(v.tensor_scalar(idxAf[:, col:col + 1], iota65[:],
                                                            sc64[0:65, 8 + p2:9 + p2], float(h * 2080) + 0.0,
                                                            ALU.add, ALU.add))
                            fnE.wait(v)
                            fnE.inc(v.tensor_copy(idxQ[:], idxQf[:]))
                            fnE.inc(v.tensor_copy(idxK[:], idxKf[:]))
                            fnE.inc(v.tensor_copy(idxA[:], idxAf[:]))
                            fnE.wait(v)
                    nc.all_engine_barrier()

                    qview = d_q1tm.ap().rearrange("t (m d) -> (t m) d", d=64)
                    kview = d_k1tm.ap().rearrange("t (m d) -> (t m) d", d=64)
                    vview2 = d_v1tm.ap().rearrange("t (m d) -> (t m) d", d=64)
                    accview = d_l1acc.ap().rearrange("r (b c) -> (r b) c", c=64)

                    with nc.Block() as b:
                        @b.gpsimd
                        def _(gp):
                            it = 0
                            for p2 in range(3):
                                for h in range(L1_H):
                                    col = p2 * 16 + h
                                    par = it % 2
                                    if it >= 2:
                                        gp.wait_ge(tpE.sem, (it - 1) * 2)
                                        gp.wait_ge(cxE.sem, it - 1)
                                    gl.inc(gp.indirect_dma_start(
                                        out=qblk[:, par * 64:(par + 1) * 64], out_offset=None,
                                        in_=qview[0:64, :], in_offset=IndirectOffsetOnAxis(idxQ[:, col:col + 1], 0)), 16)
                                    gl.inc(gp.indirect_dma_start(
                                        out=kblk[:, par * 64:(par + 1) * 64], out_offset=None,
                                        in_=kview[0:64, :], in_offset=IndirectOffsetOnAxis(idxK[:, col:col + 1], 0)), 16)
                                    gl.inc(gp.indirect_dma_start(
                                        out=vblk[:].rearrange("p (s c) -> p s c", c=65)[:, par, 0:64],
                                        out_offset=None,
                                        in_=vview2[0:64, :], in_offset=IndirectOffsetOnAxis(idxK[:, col:col + 1], 0)), 16)
                                    if it >= 2:
                                        gp.wait_ge(evE.sem, it - 1)
                                        pcol = ((it - 2) // 16) * 16 + ((it - 2) % 16)
                                        scE.inc(gp.indirect_dma_start(
                                            out=accview[:, :],
                                            out_offset=IndirectOffsetOnAxis(idxA[:, pcol:pcol + 1], 0),
                                            in_=extE[:, ((it - 2) % 2) * 64:((it - 2) % 2) * 64 + 64],
                                            in_offset=None, compute_op=ALU.add), 16)
                                    it += 1
                            for it2 in (46, 47):
                                gp.wait_ge(evE.sem, it2 + 1)
                                pcol = (it2 // 16) * 16 + (it2 % 16)
                                scE.inc(gp.indirect_dma_start(
                                    out=accview[:, :],
                                    out_offset=IndirectOffsetOnAxis(idxA[:, pcol:pcol + 1], 0),
                                    in_=extE[:, (it2 % 2) * 64:(it2 % 2) * 64 + 64],
                                    in_offset=None, compute_op=ALU.add), 16)
                            gp.wait_ge(scE.sem, scE.v)

                        @b.tensor
                        def _(t):
                            it = 0
                            for p2 in range(3):
                                for h in range(L1_H):
                                    par = it % 2
                                    t.wait_ge(gl.sem, 32 + (it + 1) * 48)
                                    if it >= 2:
                                        t.wait_ge(teE.sem, (it - 1) * 2)
                                    tpE.inc(t.transpose(psTq[0:64, 0:64], qblk[:, par * 64:(par + 1) * 64], identb[0:64, 0:64]))
                                    tpE.inc(t.transpose(psTk[0:64, 0:64], kblk[:, par * 64:(par + 1) * 64], identb[0:64, 0:64]))
                                    t.wait_ge(teE.sem, (it + 1) * 2)
                                    if it >= 1:
                                        t.wait_ge(exE.sem, it)
                                    mi = t.matmul(psE[0:64, 0:64], kfmE[:, par * 64:(par + 1) * 64],
                                                  qfmE[:, par * 64:(par + 1) * 64], start=True, stop=True)
                                    mmE.inc(mi)
                                    t.wait_ge(dmE.sem, it + 1)
                                    if it >= 2:
                                        t.wait_ge(evE.sem, it - 1)
                                    ci = t.matmul(psE2[0:65, 0:64],
                                                  vblk[:].rearrange("p (s c) -> p s c", c=65)[:, par, :],
                                                  expE[:, par * 64:(par + 1) * 64], start=True, stop=True)
                                    cxE.inc(ci)
                                    it += 1

                        @b.scalar
                        def _(s3):
                            it = 0
                            for p2 in range(3):
                                for h in range(L1_H):
                                    par = it % 2
                                    s3.wait_ge(tpE.sem, (it + 1) * 2)
                                    teE.inc(s3.activation(qfmE[:, par * 64:(par + 1) * 64], psTq[0:64, 0:64], AF.Copy))
                                    teE.inc(s3.activation(kfmE[:, par * 64:(par + 1) * 64], psTk[0:64, 0:64], AF.Copy))
                                    s3.wait_ge(mmE.sem, it + 1)
                                    exE.inc(s3.activation(expE[:, par * 64:(par + 1) * 64], psE[0:64, 0:64],
                                                          AF.Exp, bias=negMc[0:64, 0:1], scale=0.125))
                                    it += 1

                        @b.vector
                        def _(v):
                            it = 0
                            for p2 in range(3):
                                for h in range(L1_H):
                                    par = it % 2
                                    v.wait_ge(exE.sem, it + 1)
                                    dmE.inc(v.tensor_scalar(expE[:, par * 64:(par + 1) * 64],
                                                            expE[:, par * 64:(par + 1) * 64],
                                                            dcol[0:64, p2:p2 + 1], None, ALU.mult))
                                    v.wait_ge(cxE.sem, it + 1)
                                    evE.inc(v.tensor_copy(extE[:, par * 64:(par + 1) * 64], psE2[0:65, 0:64]))
                                    it += 1
                    nc.all_engine_barrier()

                with contextlib.ExitStack() as s2:
                    accb = s2.enter_context(nc.sbuf_tensor("s8f_accb", [65, T], F32))
                    rrow = s2.enter_context(nc.sbuf_tensor("s8f_rrow", [1, T], F32))
                    ctxn = s2.enter_context(nc.sbuf_tensor("s8f_ctxn", [64, T], BF16))
                    wot = s2.enter_context(nc.sbuf_tensor("s8f_wot", [64, 16 * D], BF16))
                    bor = s2.enter_context(nc.sbuf_tensor("s8f_bor", [1, D], BF16))
                    att = s2.enter_context(nc.sbuf_tensor("s8f_att", [128, D], F32))
                    psB = s2.enter_context(nc.psum_tensor("s8f_psB", [128, 512], F32))
                    hl2 = Ctr(s2.enter_context(nc.semaphore("s8fh")))
                    dv2 = Ctr(s2.enter_context(nc.semaphore("s8fd")))
                    pj2 = Ctr(s2.enter_context(nc.semaphore("s8fp")))
                    ev2 = Ctr(s2.enter_context(nc.semaphore("s8fe")))
                    st2 = Ctr(s2.enter_context(nc.semaphore("s8fs")))
                    fn2 = Ctr(s2.enter_context(nc.semaphore("s8ff")))
                    with nc.Block() as b:
                        @b.gpsimd
                        def _(gp):
                            gp.dma_start(out=wot[:].rearrange("p (c o) -> p c o", o=D),
                                         in_=w_l1wot.rearrange("(c p) o -> p c o", p=64)).then_inc(hl2.sem, 16)
                            gp.dma_start(out=bor[:], in_=w_l1bo[:, :]).then_inc(hl2.sem, 16)
                            hl2.v += 32
                            hl2.wait(gp)
                    nc.all_engine_barrier()
                    for h in range(L1_H):
                        base_h = hl2.v
                        with nc.Block() as b:
                            @b.sync
                            def _(sync):
                                hl2.inc(sync.dma_start(out=accb[:], in_=d_l1acc.ap()[h * 65:(h + 1) * 65, :]), 16)
                                hl2.wait(sync)
                                hl2.inc(sync.dma_start(out=rrow[:], in_=accb[64:65, :]), 16)

                            @b.vector
                            def _(v):
                                hl2.wait(v)
                                fn2.inc(v.reciprocal(rrow[:], rrow[:]))
                                fn2.wait(v)
                        nc.all_engine_barrier()
                        with nc.Block() as b:
                            @b.tensor
                            def _(t):
                                for ts2 in range(4):
                                    if ts2 >= 2:
                                        t.wait_ge(dv2.sem, dv2.v + ts2 - 1)
                                    pb = ps0 if ts2 % 2 == 0 else ps1
                                    pj2.inc(t.matmul(pb[0:64, 0:512], ones1f[:, 0:64],
                                                     rrow[0:1, ts2 * 512:(ts2 + 1) * 512],
                                                     start=True, stop=True))

                            @b.vector
                            def _(v):
                                for ts2 in range(4):
                                    v.wait_ge(pj2.sem, pj2.v - 4 + ts2 + 1)
                                    pb = ps0 if ts2 % 2 == 0 else ps1
                                    dv2.inc(v.tensor_mul(ctxn[:, ts2 * 512:(ts2 + 1) * 512],
                                                         accb[0:64, ts2 * 512:(ts2 + 1) * 512],
                                                         pb[0:64, 0:512]))
                        nc.all_engine_barrier()
                        with nc.Block() as b:
                            ev2b = ev2.v

                            @b.tensor
                            def _(t):
                                for os2 in range(2):
                                    if os2 == 1:
                                        t.wait_ge(ev2.sem, ev2b + 1)
                                    for m in range(16):
                                        t.matmul(psB[:, 0:512],
                                                 ctxn[:].rearrange("d (t m) -> d m t", m=16)[:, m, :],
                                                 wot[:].rearrange("p (c o) -> p c o", o=D)[:, m, os2 * 512:(os2 + 1) * 512],
                                                 start=(m == 0), stop=False)
                                    mmi = t.matmul(psB[:, 0:512], ones1b[:], bor[:, os2 * 512:(os2 + 1) * 512],
                                                   start=False, stop=True)
                                    pj2.inc(mmi)

                            @b.vector
                            def _(v):
                                for os2 in range(2):
                                    v.wait_ge(pj2.sem, pj2.v - 2 + os2 + 1)
                                    ev2.inc(v.tensor_copy(att[:, os2 * 512:(os2 + 1) * 512], psB[:, 0:512]))

                            @b.gpsimd
                            def _(gp):
                                gp.wait_ge(ev2.sem, ev2.v)
                                st2.inc(gp.dma_start(out=d_attn1.ap()[h * 128:(h + 1) * 128, :], in_=att[:]), 16)
                                gp.wait_ge(st2.sem, st2.v)
                        nc.all_engine_barrier()

        if want("s6"):
            with contextlib.ExitStack() as st:
                xh = st.enter_context(nc.sbuf_tensor("s6_xh", [128, ND * T], BF16))
                xl = st.enter_context(nc.sbuf_tensor("s6_xl", [128, ND * T], BF16))
                rwf = st.enter_context(nc.sbuf_tensor("s6_rwf", [128, ND * 8], F32))
                rwh = st.enter_context(nc.sbuf_tensor("s6_rwh", [128, ND * 8], BF16))
                rwl = st.enter_context(nc.sbuf_tensor("s6_rwl", [128, ND * 8], BF16))
                rwt2 = st.enter_context(nc.sbuf_tensor("s6_rwt2", [128, ND * 8], F32))
                rbc = st.enter_context(nc.sbuf_tensor("s6_rbc", [8, 1], F32))
                lgT = st.enter_context(nc.sbuf_tensor("s6_lgT", [8, T], F32))
                lgtm = st.enter_context(nc.sbuf_tensor("s6_lgtm", [128, NT * 8], F32))
                ps0 = st.enter_context(nc.psum_tensor("s6_ps0", [128, 512], F32))
                ps1 = st.enter_context(nc.psum_tensor("s6_ps1", [128, 512], F32))
                ld = Ctr(st.enter_context(nc.semaphore("s6ld")))
                pj = Ctr(st.enter_context(nc.semaphore("s6pj")))
                ev = Ctr(st.enter_context(nc.semaphore("s6ev")))
                fn = Ctr(st.enter_context(nc.semaphore("s6fn")))
                cc = Ctr(st.enter_context(nc.semaphore("s6cc")))

                with nc.Block() as b:
                    @b.sync
                    def _(sync):
                        for c in range(ND):
                            ld.inc(sync.dma_start_transpose(xh[:, c * T:(c + 1) * T],
                                                            d_xn1tm.ap()[:, c * 128:(c + 1) * 128]), 16)
                        for c in range(ND):
                            ld.inc(sync.dma_start_transpose(xl[:, c * T:(c + 1) * T],
                                                            d_xn1lo.ap()[:, c * 128:(c + 1) * 128]), 16)

                    @b.gpsimd
                    def _(gp):
                        ld.inc(gp.dma_start(out=rwf[:].rearrange("p (c e) -> p c e", e=8),
                                            in_=w_rwt.rearrange("(c p) e -> p c e", p=128)), 16)
                        ld.inc(gp.dma_start(out=rbc[:, 0], in_=w_rb[0, :]), 16)
                        ld.wait(gp)
                nc.all_engine_barrier()
                with nc.Block() as b:
                    @b.vector
                    def _(v):
                        fn.inc(v.tensor_copy(rwh[:], rwf[:]))
                        fn.wait(v)
                        fn.inc(v.tensor_copy(rwt2[:], rwh[:]))
                        fn.wait(v)
                        fn.inc(v.tensor_sub(rwt2[:], rwf[:], rwt2[:]))
                        fn.wait(v)
                        v.tensor_copy(rwl[:], rwt2[:])
                nc.all_engine_barrier()
                with nc.Block() as b:
                    @b.tensor
                    def _(t):
                        for ts2 in range(4):
                            pb = ps0 if ts2 % 2 == 0 else ps1
                            if ts2 >= 2:
                                t.wait_ge(ev.sem, ts2 - 1)
                            first = True
                            for (xa, wa) in ((xh, rwh), (xh, rwl), (xl, rwh), (xl, rwl)):
                                for dc in range(ND):
                                    mmi = t.matmul(pb[0:8, 0:512],
                                                   wa[:, dc * 8:(dc + 1) * 8],
                                                   xa[:, dc * T + ts2 * 512: dc * T + ts2 * 512 + 512],
                                                   start=first, stop=(dc == ND - 1 and wa is rwl and xa is xl))
                                    first = False
                            pj.inc(mmi)

                    @b.scalar
                    def _(s3):
                        for ts2 in range(4):
                            s3.wait_ge(pj.sem, ts2 + 1)
                            pb = ps0 if ts2 % 2 == 0 else ps1
                            ev.inc(s3.activation(lgT[:, ts2 * 512:(ts2 + 1) * 512], pb[0:8, 0:512],
                                                 AF.Identity, bias=rbc[:, 0:1]))
                nc.all_engine_barrier()
                with nc.Block() as b:
                    @b.tensor
                    def _(t):
                        for i in range(NT):
                            if i >= 2:
                                t.wait_ge(ev.sem, 4 + i - 1)
                            pb = ps0 if i % 2 == 0 else ps1
                            pj.inc(t.transpose(pb[0:128, 0:8], lgT[:, i * 128:(i + 1) * 128], ident[0:8, 0:8]))

                    @b.vector
                    def _(v):
                        for i in range(NT):
                            v.wait_ge(pj.sem, 4 + i + 1)
                            pb = ps0 if i % 2 == 0 else ps1
                            ev.inc(v.tensor_copy(lgtm[:, i * 8:(i + 1) * 8], pb[0:128, 0:8]))
                nc.all_engine_barrier()
                with nc.Block() as b:
                    @b.gpsimd
                    def _(gp):
                        gp.dma_start(out=c_lg_in.ap().rearrange("(i p) e -> p i e", p=128),
                                     in_=lgtm[:].rearrange("p (i e) -> p i e", e=8)).then_inc(ld.sem, 16)
                        gp.dma_start(out=c_xn_in.ap(), in_=d_xn1tm.ap()).then_inc(ld.sem, 16)
                        ld.v += 32
                        ld.wait(gp)
                        gp.collective_compute(
                            "AllGather", ALU.bypass,
                            replica_groups=[list(range(8))],
                            ins=[c_lg_in.ap()], outs=[c_lg_out.ap()],
                        ).then_inc(cc.sem)
                        gp.collective_compute(
                            "AllGather", ALU.bypass,
                            replica_groups=[list(range(8))],
                            ins=[c_xn_in.ap()], outs=[c_xf_out.ap()],
                        ).then_inc(ccx)
                        cc.v += 1
                        cc.wait(gp)
                        if d_lgdbg is not None:
                            gp.dma_start(out=d_lgdbg.ap(), in_=c_lg_out.ap()).then_inc(ld.sem, 16)
                            ld.v += 16
                            ld.wait(gp)
                nc.all_engine_barrier()

        if want("s7"):
            with contextlib.ExitStack() as st:
                lg = st.enter_context(nc.sbuf_tensor("s7_lg", [128, 1024], F32))
                ex = st.enter_context(nc.sbuf_tensor("s7_ex", [128, 2 * 1024], F32))
                zz = st.enter_context(nc.sbuf_tensor("s7_z", [128, 2 * 128], F32))
                m1 = st.enter_context(nc.sbuf_tensor("s7_m1", [128, 2 * 128], F32))
                i1 = st.enter_context(nc.sbuf_tensor("s7_i1", [128, 2 * 128], F32))
                m2 = st.enter_context(nc.sbuf_tensor("s7_m2", [128, 2 * 128], F32))
                i2 = st.enter_context(nc.sbuf_tensor("s7_i2", [128, 2 * 128], F32))
                w1 = st.enter_context(nc.sbuf_tensor("s7_w1", [128, 2 * 128], F32))
                w2 = st.enter_context(nc.sbuf_tensor("s7_w2", [128, 2 * 128], F32))
                tq = st.enter_context(nc.sbuf_tensor("s7_tq", [128, 2 * 128], F32))
                tq2 = st.enter_context(nc.sbuf_tensor("s7_tq2", [128, 2 * 128], F32))
                If = st.enter_context(nc.sbuf_tensor("s7_If", [128, 2048], F32))
                Ib = st.enter_context(nc.sbuf_tensor("s7_Ib", [128, 2048], BF16))
                utri = st.enter_context(nc.sbuf_tensor("s7_utri", [128, 128], BF16))
                rkin = st.enter_context(nc.sbuf_tensor("s7_rk", [128, 2048], F32))
                ctotT = st.enter_context(nc.sbuf_tensor("s7_ctT", [128, 16], F32))
                ctp = st.enter_context(nc.sbuf_tensor("s7_ctp", [128, 16], F32))
                ctpr = st.enter_context(nc.sbuf_tensor("s7_ctpr", [16, 128], F32))
                crow = st.enter_context(nc.sbuf_tensor("s7_crow", [1, 2048], F32))
                cpB = st.enter_context(nc.sbuf_tensor("s7_cpB", [128, 2048], F32))
                usf = st.enter_context(nc.sbuf_tensor("s7_usf", [128, 128], F32))
                onesc = st.enter_context(nc.sbuf_tensor("s7_onesc", [128, 128], F32))
                pos_own = st.enter_context(nc.sbuf_tensor("s7_po", [128, 256], F32))
                keep_own = st.enter_context(nc.sbuf_tensor("s7_ko", [128, 256], F32))
                kfull = st.enter_context(nc.sbuf_tensor("s7_kf", [128, 2048], F32))
                offs = st.enter_context(nc.sbuf_tensor("s7_offs", [128, 256], F32))
                offi = st.enter_context(nc.sbuf_tensor("s7_offi", [128, 256], I32))
                wt = st.enter_context(nc.sbuf_tensor("s7_wt", [128, 256], F32))
                tvals = st.enter_context(nc.sbuf_tensor("s7_tv", [128, 128], F32))
                prs = st.enter_context(nc.sbuf_tensor("s7_prs", [128, 512], F32))
                offsG = st.enter_context(nc.sbuf_tensor("s7_offsG", [128, 256], F32))
                sbT = st.enter_context(nc.sbuf_tensor("s7_sbT", [128, 512], F32))
                occ = st.enter_context(nc.sbuf_tensor("s7_occ", [128, 1], I32))
                tv2s = st.enter_context(nc.sbuf_tensor("s7_tv2", [128, 32], F32))
                g1s = st.enter_context(nc.sbuf_tensor("s7_g1", [128, 128], F32))
                wo2 = st.enter_context(nc.sbuf_tensor("s7_wo2", [128, 64], F32))
                offi2 = st.enter_context(nc.sbuf_tensor("s7_offi2", [128, 32], I32))
                prs2 = st.enter_context(nc.sbuf_tensor("s7_prs2", [128, 64], F32))
                zrow = st.enter_context(nc.sbuf_tensor("s7_zrow", [128, 322], F32))
                ps0 = st.enter_context(nc.psum_tensor("s7_ps0", [128, 512], F32))
                ps1 = st.enter_context(nc.psum_tensor("s7_ps1", [128, 512], F32))
                ps2 = st.enter_context(nc.psum_tensor("s7_ps2", [128, 512], F32))
                ps3 = st.enter_context(nc.psum_tensor("s7_ps3", [128, 512], F32))
                ld = Ctr(st.enter_context(nc.semaphore("s7ld")))
                pj = Ctr(st.enter_context(nc.semaphore("s7pj")))
                ev = Ctr(st.enter_context(nc.semaphore("s7ev")))
                fn = Ctr(st.enter_context(nc.semaphore("s7fn")))
                sc = Ctr(st.enter_context(nc.semaphore("s7sc")))
                cc7 = Ctr(st.enter_context(nc.semaphore("s7cc")))

                with nc.Block() as b:
                    @b.sync
                    def _(sync):
                        ld.inc(sync.dma_start(
                            out=lg[:].rearrange("p (c e) -> p c e", e=8),
                            in_=c_lg_out.ap().rearrange("(c p) e -> p c e", p=128)), 16)
                        ld.inc(sync.dma_start(out=occ[0:64, 0:1], in_=w_owncols[:, :]), 16)
                        ld.inc(sync.dma_start(out=tv2s[:], in_=w_tv2[:, :]), 16)

                    @b.gpsimd
                    def _(gp):
                        gp.memset(zrow[:], 0.0)
                        gp.iota(tvals[:], [[128, 128]], channel_multiplier=1,
                                allow_small_or_imprecise_dtypes=True)
                        ld.inc(gp.dma_start(out=utri[:], in_=c_utri[:, :]), 16)
                        ld.inc(gp.dma_start(out=usf[:], in_=c_usf[:, :]), 16)
                        gp.memset(onesc[:], 1.0)
                        gp.dma_start(out=d_G.ap().rearrange("(p a) f -> p a f", p=128),
                                     in_=zrow[:, 0:322].rearrange("p (a f) -> p a f", f=2)).then_inc(ld.sem, 16)
                        ld.v += 16
                        ld.wait(gp)
                nc.all_engine_barrier()
                with nc.Block() as b:
                    @b.scalar
                    def _(s3):
                        s3.activation(ex[:, 0:1024], lg[:], AF.Exp)
                nc.all_engine_barrier()
                with nc.Block() as b:
                    @b.vector
                    def _(v):
                        v.reduce_sum(zz[:, 0:128], ex[:, 0:1024].rearrange("p (c e) -> p c e", e=8), AX.X)
                        fn.inc(v.reduce_max(m1[:, 0:128], ex[:, 0:1024].rearrange("p (c e) -> p c e", e=8), AX.X))
                        fn.wait(v)
                        fn.inc(v.tensor_scalar(i1[:, 0:128], m1[:, 0:128], 0.0, 8.0, ALU.mult, ALU.add))
                        fn.wait(v)
                        for e in range(8):
                            sl2 = ex[:, 0:1024].rearrange("p (c e) -> p c e", e=8)[:, :, e]
                            fn.inc(v.tensor_tensor(tq[:, 0:128], sl2, m1[:, 0:128], ALU.is_equal))
                            fn.wait(v)
                            fn.inc(v.tensor_scalar(tq[:, 0:128], tq[:, 0:128], float(e - 8), 8.0, ALU.mult, ALU.add))
                            fn.wait(v)
                            fn.inc(v.tensor_tensor(i1[:, 0:128], i1[:, 0:128], tq[:, 0:128], ALU.min))
                            fn.wait(v)
                        fn.inc(v.tensor_scalar(m2[:, 0:128], m1[:, 0:128], 0.0, -1e30, ALU.mult, ALU.add))
                        fn.inc(v.tensor_scalar(i2[:, 0:128], m1[:, 0:128], 0.0, 8.0, ALU.mult, ALU.add))
                        fn.wait(v)
                        for e in range(8):
                            sl2 = ex[:, 0:1024].rearrange("p (c e) -> p c e", e=8)[:, :, e]
                            fn.inc(v.tensor_scalar(tq[:, 0:128], i1[:, 0:128], float(e), None, ALU.is_equal))
                            fn.wait(v)
                            fn.inc(v.tensor_scalar(tq[:, 0:128], tq[:, 0:128], -1e30, None, ALU.mult))
                            fn.wait(v)
                            fn.inc(v.tensor_tensor(tq[:, 0:128], sl2, tq[:, 0:128], ALU.add))
                            fn.wait(v)
                            fn.inc(v.tensor_tensor(m2[:, 0:128], m2[:, 0:128], tq[:, 0:128], ALU.max))
                            fn.wait(v)
                        for e in range(8):
                            sl2 = ex[:, 0:1024].rearrange("p (c e) -> p c e", e=8)[:, :, e]
                            fn.inc(v.tensor_tensor(tq[:, 0:128], sl2, m2[:, 0:128], ALU.is_equal))
                            fn.wait(v)
                            fn.inc(v.tensor_scalar(tq2[:, 0:128], i1[:, 0:128], float(e), None, ALU.not_equal))
                            fn.wait(v)
                            fn.inc(v.tensor_tensor(tq[:, 0:128], tq[:, 0:128], tq2[:, 0:128], ALU.mult))
                            fn.wait(v)
                            fn.inc(v.tensor_scalar(tq[:, 0:128], tq[:, 0:128], float(e - 8), 8.0, ALU.mult, ALU.add))
                            fn.wait(v)
                            fn.inc(v.tensor_tensor(i2[:, 0:128], i2[:, 0:128], tq[:, 0:128], ALU.min))
                            fn.wait(v)
                        fn.inc(v.tensor_tensor(tq[:, 0:128], m1[:, 0:128], m2[:, 0:128], ALU.add))
                        fn.inc(v.tensor_scalar(tq2[:, 0:128], zz[:, 0:128], MOE_EPS, None, ALU.mult))
                        fn.wait(v)
                        fn.inc(v.tensor_tensor(tq[:, 0:128], tq[:, 0:128], tq2[:, 0:128], ALU.add))
                        fn.wait(v)
                        fn.inc(v.reciprocal(tq[:, 0:128], tq[:, 0:128]))
                        fn.wait(v)
                        fn.inc(v.tensor_tensor(w1[:, 0:128], m1[:, 0:128], tq[:, 0:128], ALU.mult))
                        fn.inc(v.tensor_tensor(w2[:, 0:128], m2[:, 0:128], tq[:, 0:128], ALU.mult))
                        fn.wait(v)
                nc.all_engine_barrier()
                with nc.Block() as b:
                    @b.vector
                    def _(v):
                        fn.inc(v.tensor_scalar(tq2[:, 0:128], w2[:, 0:128], MOE_EPS, None, ALU.is_gt))
                        fn.wait(v)
                        for e in range(8):
                            fn.inc(v.tensor_scalar(If[:, e * 256: e * 256 + 128], i1[:, 0:128],
                                                   float(e), None, ALU.is_equal))
                            fn.inc(v.tensor_scalar(If[:, e * 256 + 128: e * 256 + 256], i2[:, 0:128],
                                                   float(e), None, ALU.is_equal))
                            fn.wait(v)
                            fn.inc(v.tensor_tensor(If[:, e * 256 + 128: e * 256 + 256],
                                                   If[:, e * 256 + 128: e * 256 + 256], tq2[:, 0:128], ALU.mult))
                            fn.wait(v)
                        v.tensor_copy(Ib[:], If[:])
                nc.all_engine_barrier()
                with nc.Block() as b:
                    @b.tensor
                    def _(t):
                        for q4 in range(4):
                            pj.inc(t.matmul([ps0, ps1, ps2, ps3][q4][:, 0:512], utri[:],
                                            Ib[:, q4 * 512:(q4 + 1) * 512], start=True, stop=True))

                    @b.vector
                    def _(v):
                        for q4 in range(4):
                            v.wait_ge(pj.sem, pj.v - 4 + q4 + 1)
                            ev.inc(v.tensor_copy(rkin[:, q4 * 512:(q4 + 1) * 512],
                                                 [ps0, ps1, ps2, ps3][q4][:, 0:512]))
                nc.all_engine_barrier()
                with nc.Block() as b:
                    @b.sync
                    def _(sync):
                        ld.inc(sync.dma_start(out=d_rows2.ap()[1:2, :], in_=rkin[127:128, :]), 16)
                        ld.wait(sync)
                        ld.inc(sync.dma_start(out=crow[:], in_=d_rows2.ap()[1:2, :]), 16)
                        ld.wait(sync)
                nc.all_engine_barrier()
                with nc.Block() as b:
                    @b.tensor
                    def _(t):
                        for j in range(16):
                            pb = ps0 if j % 2 == 0 else ps1
                            if j >= 2:
                                t.wait_ge(ev.sem, ev.v + j - 1)
                            pj.inc(t.transpose(pb[0:128, 0:1], crow[0:1, j * 128:(j + 1) * 128], ident[0:1, 0:1]))

                    @b.vector
                    def _(v):
                        evb7 = ev.v
                        for j in range(16):
                            v.wait_ge(pj.sem, pj.v - 16 + j + 1)
                            pb = ps0 if j % 2 == 0 else ps1
                            ev.inc(v.tensor_copy(ctotT[:, j:j + 1], pb[0:128, 0:1]))
                nc.all_engine_barrier()
                with nc.Block() as b:
                    @b.tensor
                    def _(t):
                        t.matmul(ps0[:, 0:16], usf[:], ctotT[:], start=True, stop=True)
                        t.matmul(ps2[:, 0:8], onesc[:],
                                 ctotT[:].rearrange("p (e h) -> p e h", h=2)[:, :, 0],
                                 start=True, stop=True)
                nc.all_engine_barrier()
                with nc.Block() as b:
                    @b.vector
                    def _(v):
                        fn.inc(v.tensor_copy(ctp[:], ps0[:, 0:16]))
                        fn.wait(v)
                        fn.inc(v.tensor_tensor(ctp[:].rearrange("p (e h) -> p e h", h=2)[:, :, 1],
                                               ctp[:].rearrange("p (e h) -> p e h", h=2)[:, :, 1],
                                               ps2[:, 0:8], ALU.add))
                        fn.wait(v)
                nc.all_engine_barrier()
                with nc.Block() as b:
                    @b.tensor
                    def _(t):
                        t.transpose(ps1[0:16, 0:128], ctp[:], ident[:])
                nc.all_engine_barrier()
                with nc.Block() as b:
                    @b.vector
                    def _(v):
                        v.tensor_copy(ctpr[:], ps1[0:16, 0:128])
                nc.all_engine_barrier()
                with nc.Block() as b:
                    @b.sync
                    def _(sync):
                        ld.inc(sync.dma_start(out=d_rows2.ap()[1:2, :].rearrange("o (eh p) -> (o eh) p", p=128),
                                              in_=ctpr[:]), 16)
                        ld.wait(sync)
                        ld.inc(sync.dma_start(out=crow[:], in_=d_rows2.ap()[1:2, :]), 16)
                        ld.wait(sync)
                nc.all_engine_barrier()
                with nc.Block() as b:
                    @b.tensor
                    def _(t):
                        for q4 in range(4):
                            pj.inc(t.matmul([ps0, ps1, ps2, ps3][q4][:, 0:512], ones1f[:],
                                            crow[0:1, q4 * 512:(q4 + 1) * 512], start=True, stop=True))

                    @b.vector
                    def _(v):
                        for q4 in range(4):
                            v.wait_ge(pj.sem, pj.v - 4 + q4 + 1)
                            ev.inc(v.tensor_copy(cpB[:, q4 * 512:(q4 + 1) * 512],
                                                 [ps0, ps1, ps2, ps3][q4][:, 0:512]))
                nc.all_engine_barrier()
                with nc.Block() as b:
                    @b.vector
                    def _(v):
                        fn.inc(v.tensor_add(rkin[:], rkin[:], cpB[:]))
                        fn.wait(v)
                        fn.inc(v.tensor_sub(rkin[:], rkin[:], If[:]))
                        fn.wait(v)
                        fn.inc(v.tensor_scalar(kfull[:], rkin[:], float(CAPACITY), None, ALU.is_lt))
                        fn.wait(v)
                        fn.inc(v.tensor_tensor(kfull[:], kfull[:], If[:], ALU.mult))
                        fn.wait(v)
                        fn.inc(v.tensor_tensor(cpB[:], rkin[:], If[:], ALU.mult))
                        fn.wait(v)
                        fn.inc(v.reduce_sum(pos_own[:], cpB[:].rearrange("p (e c) -> p c e", e=8), AX.X))
                        fn.inc(v.reduce_sum(keep_own[:], kfull[:].rearrange("p (e c) -> p c e", e=8), AX.X))
                        fn.wait(v)
                        fn.inc(v.tensor_scalar(offs[:, 0:128], i1[:, 0:128], float(CAP_PAD), None, ALU.mult))
                        fn.inc(v.tensor_scalar(offs[:, 128:256], i2[:, 0:128], float(CAP_PAD), None, ALU.mult))
                        fn.wait(v)
                        fn.inc(v.tensor_add(offs[:], offs[:], pos_own[:]))
                        fn.wait(v)
                        fn.inc(v.tensor_tensor(offs[:], offs[:], keep_own[:], ALU.mult))
                        fn.wait(v)
                        fn.inc(v.tensor_scalar(tq[:, 0:256], keep_own[:], -20480.0, 20480.0, ALU.mult, ALU.add))
                        fn.wait(v)
                        fn.inc(v.tensor_add(offs[:], offs[:], tq[:, 0:256]))
                        fn.wait(v)
                        fn.inc(v.tensor_copy(offi[:], offs[:]))
                        fn.inc(v.tensor_copy(offsG[:], offs[:]))
                        fn.wait(v)
                        fn.inc(v.tensor_tensor(offs[:], offs[:], keep_own[:], ALU.mult))
                        fn.inc(v.tensor_tensor(wt[:, 0:128], w1[:, 0:128], keep_own[:, 0:128], ALU.mult))
                        fn.inc(v.tensor_tensor(wt[:, 128:256], w2[:, 0:128], keep_own[:, 128:256], ALU.mult))
                        fn.wait(v)
                        for half in range(2):
                            fn.inc(v.tensor_copy(prs[:, half * 256:(half + 1) * 256]
                                                 .rearrange("p (c f) -> p c f", f=2)[:, :, 0], tvals[:]))
                            fn.inc(v.tensor_copy(prs[:, half * 256:(half + 1) * 256]
                                                 .rearrange("p (c f) -> p c f", f=2)[:, :, 1],
                                                 wt[:, half * 128:(half + 1) * 128]))
                        fn.wait(v)
                        for half in range(2):
                            fn.inc(v.tensor_copy(ex[:, 1024 + half * 256: 1024 + (half + 1) * 256]
                                                 .rearrange("p (c f) -> p c f", f=2)[:, :, 0],
                                                 wt[:, half * 128:(half + 1) * 128]))
                            fn.inc(v.tensor_copy(ex[:, 1024 + half * 256: 1024 + (half + 1) * 256]
                                                 .rearrange("p (c f) -> p c f", f=2)[:, :, 1],
                                                 offs[:, half * 128:(half + 1) * 128]))
                        fn.wait(v)
                nc.all_engine_barrier()
                if d_dbg7 is not None:
                    with nc.Block() as b:
                        @b.sync
                        def _(sync):
                            sync.dma_start(out=d_dbg7.ap()[:, 0:128], in_=i1[:, 0:128]).then_inc(ld.sem, 16)
                            sync.dma_start(out=d_dbg7.ap()[:, 128:256], in_=i2[:, 0:128]).then_inc(ld.sem, 16)
                            sync.dma_start(out=d_dbg7.ap()[:, 256:384], in_=w1[:, 0:128]).then_inc(ld.sem, 16)
                            sync.dma_start(out=d_dbg7.ap()[:, 384:512], in_=w2[:, 0:128]).then_inc(ld.sem, 16)
                            sync.dma_start(out=d_dbg7.ap()[:, 512:768], in_=pos_own[:]).then_inc(ld.sem, 16)
                            sync.dma_start(out=d_dbg7.ap()[:, 768:1024], in_=keep_own[:]).then_inc(ld.sem, 16)
                            sync.dma_start(out=d_dbg7.ap()[:, 1024:1152], in_=m1[:, 0:128]).then_inc(ld.sem, 16)
                            sync.dma_start(out=d_dbg7.ap()[:, 1152:1280], in_=zz[:, 0:128]).then_inc(ld.sem, 16)
                            sync.dma_start(out=d_dbg7.ap()[:, 2048:3072], in_=ex[:, 0:1024]).then_inc(ld.sem, 16)
                            sync.dma_start(out=d_dbg7.ap()[:, 3072:4096], in_=lg[:]).then_inc(ld.sem, 16)
                            ld.v += 10 * 16
                            ld.wait(sync)
                    nc.all_engine_barrier()
                with nc.Block() as b:
                    @b.sync
                    def _(sync):
                        ld.inc(sync.dma_start(
                            out=d_T.ap().rearrange("(p hc) f -> p (hc f)", p=128),
                            in_=ex[:, 1024:1536]), 16)
                nc.all_engine_barrier()
                with nc.Block() as b:
                    @b.tensor
                    def _(t):
                        pj.inc(t.transpose(ps0[:, 0:128], wt[:, 0:128], ident[:]))
                        pj.inc(t.transpose(ps1[:, 0:128], wt[:, 128:256], ident[:]))
                        pj.inc(t.transpose(ps2[:, 0:128], offsG[:, 0:128], ident[:]))
                        pj.inc(t.transpose(ps3[:, 0:128], offsG[:, 128:256], ident[:]))

                    @b.vector
                    def _(v):
                        for q4 in range(4):
                            v.wait_ge(pj.sem, pj.v - 4 + q4 + 1)
                            ev.inc(v.tensor_copy(sbT[:, q4 * 128:(q4 + 1) * 128],
                                                 [ps0, ps1, ps2, ps3][q4][:, 0:128]))
                nc.all_engine_barrier()
                with nc.Block() as b:
                    @b.sync
                    def _(sync):
                        ld.inc(sync.dma_start(
                            out=d_wtT.ap().rearrange("(q h) p -> h q p", h=128),
                            in_=sbT[:].rearrange("h (q p) -> h q p", q=4)), 16)
                        ld.wait(sync)

                    @b.gpsimd
                    def _(gp):
                        ld.wait(gp)
                        sc.inc(gp.indirect_dma_start(
                            out=g1s[0:64, :], out_offset=None,
                            in_=d_wtT.ap(),
                            in_offset=IndirectOffsetOnAxis(occ[0:64, 0:1], 0)), 16)
                        sc.wait(gp)
                nc.all_engine_barrier()
                with nc.Block() as b:
                    @b.tensor
                    def _(t):
                        pj.inc(t.transpose(ps0[:, 0:64], g1s[0:64, :], ident[0:64, 0:64]))

                    @b.vector
                    def _(v):
                        v.wait_ge(pj.sem, pj.v)
                        fn.inc(v.tensor_copy(wo2[:], ps0[:, 0:64]))
                        fn.wait(v)
                        fn.inc(v.tensor_copy(offi2[:], wo2[:, 32:64]))
                        fn.inc(v.tensor_copy(prs2[:].rearrange("p (c f) -> p c f", f=2)[:, :, 0], tv2s[:]))
                        fn.inc(v.tensor_copy(prs2[:].rearrange("p (c f) -> p c f", f=2)[:, :, 1], wo2[:, 0:32]))
                        fn.wait(v)
                nc.all_engine_barrier()
                with nc.Block() as b:
                    @b.gpsimd
                    def _(gp):
                        for j in range(32):
                            sc.inc(gp.indirect_dma_start(
                                out=d_G.ap(),
                                out_offset=IndirectOffsetOnAxis(offi2[:, j:j + 1], 0),
                                in_=prs2[:, j * 2:j * 2 + 2],
                                in_offset=None), 16)
                        gp.wait_ge(sc.sem, sc.v)
                        ld.wait(gp)
                        gp.collective_compute(
                            "AllReduce", ALU.add,
                            replica_groups=[list(range(8))],
                            ins=[d_G.ap()], outs=[c_G_out.ap()],
                        ).then_inc(cc7.sem)
                        cc7.v += 1
                        cc7.wait(gp)
                        if d_Gdbg is not None:
                            gp.dma_start(out=d_Gdbg.ap(), in_=c_G_out.ap()[0:N_EXP * CAP_PAD, :]).then_inc(ld.sem, 16)
                            ld.v += 16
                            ld.wait(gp)
                nc.all_engine_barrier()
        if want("s9"):
            with contextlib.ExitStack() as st:
                gidx = st.enter_context(nc.sbuf_tensor("s9_gidx", [128, NTE], I32))
                ge = st.enter_context(nc.sbuf_tensor("s9_ge", [128, 2 * NTE], F32))
                tvi = st.enter_context(nc.sbuf_tensor("s9_tvi", [128, NTE], I32))
                tstg = st.enter_context(nc.sbuf_tensor("s9_tstg", [128, 2 * D], BF16))
                ld = Ctr(st.enter_context(nc.semaphore("s9ld")))
                gl = Ctr(st.enter_context(nc.semaphore("s9gl")))
                so = Ctr(st.enter_context(nc.semaphore("s9so")))

                with nc.Block() as b:
                    @b.sync
                    def _(sync):
                        ld.inc(sync.dma_start(out=gidx[:], in_=w_exprows[:, :]), 16)

                    @b.gpsimd
                    def _(gp):
                        ld.wait(gp)
                        for j in range(NTE):
                            gl.inc(gp.indirect_dma_start(
                                out=ge[:].rearrange("p (j f) -> p j f", f=2)[:, j, :],
                                out_offset=None,
                                in_=c_G_out.ap(),
                                in_offset=IndirectOffsetOnAxis(gidx[:, j:j + 1], 0)), 16)
                        gl.wait(gp)
                nc.all_engine_barrier()
                with nc.Block() as b:
                    @b.vector
                    def _(v):
                        v.tensor_copy(tvi[:], ge[:].rearrange("p (j f) -> p j f", f=2)[:, :, 0])
                nc.all_engine_barrier()
                glb9 = gl.v
                with nc.Block() as b:
                    @b.gpsimd
                    def _(gp):
                        gp.wait_ge(ccx, 1)
                        for j in range(NTE):
                            if j >= 2:
                                gp.wait_ge(so.sem, (j - 1) * 16)
                            gl.inc(gp.indirect_dma_start(
                                out=tstg[:, (j % 2) * D:(j % 2) * D + D],
                                out_offset=None,
                                in_=c_xf_out.ap(),
                                in_offset=IndirectOffsetOnAxis(tvi[:, j:j + 1], 0)), 16)

                    @b.sync
                    def _(sync):
                        for j in range(NTE):
                            sync.wait_ge(gl.sem, glb9 + (j + 1) * 16)
                            so.inc(sync.dma_start(out=d_tok.ap()[j * 128:(j + 1) * 128, :],
                                                  in_=tstg[:, (j % 2) * D:(j % 2) * D + D]), 16)
                        sync.wait_ge(so.sem, so.v)
                nc.all_engine_barrier()

                with contextlib.ExitStack() as s2:
                    tfm = s2.enter_context(nc.sbuf_tensor("s9_tfm", [128, ND * CAP_PAD], BF16))
                    w1m = s2.enter_context(nc.sbuf_tensor("s9_w1m", [128, 2 * ND * 128], BF16))
                    f1b = s2.enter_context(nc.sbuf_tensor("s9_f1b", [128, 32], F32))
                    hst = s2.enter_context(nc.sbuf_tensor("s9_hst", [128, 2 * 512], BF16))
                    ps0 = s2.enter_context(nc.psum_tensor("s9_ps0", [128, 512], F32))
                    ps1 = s2.enter_context(nc.psum_tensor("s9_ps1", [128, 512], F32))
                    pj = Ctr(s2.enter_context(nc.semaphore("s9pj")))
                    ev = Ctr(s2.enter_context(nc.semaphore("s9ev")))
                    hs = Ctr(s2.enter_context(nc.semaphore("s9hs")))
                    with nc.Block() as b:
                        @b.sync
                        def _(sync):
                            for c in range(ND):
                                ld.inc(sync.dma_start_transpose(tfm[:, c * CAP_PAD:(c + 1) * CAP_PAD],
                                                                d_tok.ap()[:, c * 128:(c + 1) * 128]), 16)

                        @b.gpsimd
                        def _(gp):
                            ld.inc(gp.dma_start(out=f1b[:], in_=w_fc1b[:, :]), 16)
                            ld.wait(gp)
                    nc.all_engine_barrier()
                    base_ld = ld.v
                    NS5 = CAP_PAD // 512
                    with nc.Block() as b:
                        @b.gpsimd
                        def _(gp):
                            for oc in range(32):
                                if oc >= 2:
                                    gp.wait_ge(pj.sem, (oc - 1) * NS5)
                                gp.dma_start(
                                    out=w1m[:, (oc % 2) * ND * 128:(oc % 2 + 1) * ND * 128].rearrange("p (c o) -> p c o", o=128),
                                    in_=w_fc1t.rearrange("(c p) o -> p c o", p=128)[:, :, oc * 128:(oc + 1) * 128]
                                ).then_inc(ld.sem, 16)
                                ld.v += 16

                        @b.tensor
                        def _(t):
                            for oc in range(32):
                                t.wait_ge(ld.sem, base_ld + (oc + 1) * 16)
                                for ts2 in range(NS5):
                                    k = oc * NS5 + ts2
                                    if k >= 2:
                                        t.wait_ge(ev.sem, k - 1)
                                    pb = ps0 if k % 2 == 0 else ps1
                                    for dc in range(ND):
                                        mmi = t.matmul(pb[:, 0:512],
                                                       w1m[:, (oc % 2) * ND * 128 + dc * 128:(oc % 2) * ND * 128 + dc * 128 + 128],
                                                       tfm[:, dc * CAP_PAD + ts2 * 512: dc * CAP_PAD + ts2 * 512 + 512],
                                                       start=(dc == 0), stop=(dc == ND - 1))
                                    pj.inc(mmi)

                        @b.scalar
                        def _(s3):
                            for oc in range(32):
                                for ts2 in range(NS5):
                                    k = oc * NS5 + ts2
                                    s3.wait_ge(pj.sem, k + 1)
                                    if k >= 2:
                                        s3.wait_ge(hs.sem, (k - 1) * 16)
                                    pb = ps0 if k % 2 == 0 else ps1
                                    ev.inc(s3.activation(hst[:, (k % 2) * 512:(k % 2) * 512 + 512],
                                                         pb[:, 0:512], AF.Gelu, bias=f1b[:, oc:oc + 1]))

                        @b.sync
                        def _(sync):
                            for oc in range(32):
                                for ts2 in range(NS5):
                                    k = oc * NS5 + ts2
                                    sync.wait_ge(ev.sem, k + 1)
                                    hs.inc(sync.dma_start(
                                        out=d_hmoe.ap()[oc * 128:(oc + 1) * 128, ts2 * 512:(ts2 + 1) * 512],
                                        in_=hst[:, (k % 2) * 512:(k % 2) * 512 + 512]), 16)
                            sync.wait_ge(hs.sem, hs.v)
                    nc.all_engine_barrier()

                with contextlib.ExitStack() as s2:
                    w2m = s2.enter_context(nc.sbuf_tensor("s9_w2m", [128, 32 * D], BF16))
                    h2t = s2.enter_context(nc.sbuf_tensor("s9_h2t", [128, 2 * 32 * 128], BF16))
                    f2br = s2.enter_context(nc.sbuf_tensor("s9_f2br", [1, D], BF16))
                    lsr = s2.enter_context(nc.sbuf_tensor("s9_lsr", [1, D], F32))
                    lsB = s2.enter_context(nc.sbuf_tensor("s9_lsB", [128, D], F32))
                    tokt = s2.enter_context(nc.sbuf_tensor("s9_tokt", [128, 2 * D], BF16))
                    tokf = s2.enter_context(nc.sbuf_tensor("s9_tokf", [128, D], F32))
                    yst = s2.enter_context(nc.sbuf_tensor("s9_yst", [128, D], F32))
                    ysb = s2.enter_context(nc.sbuf_tensor("s9_ysb", [128, D], BF16))
                    ps0 = s2.enter_context(nc.psum_tensor("s9b_ps0", [128, 512], F32))
                    ps1 = s2.enter_context(nc.psum_tensor("s9b_ps1", [128, 512], F32))
                    ps2 = s2.enter_context(nc.psum_tensor("s9b_ps2", [128, 512], F32))
                    pj = Ctr(s2.enter_context(nc.semaphore("s9bpj")))
                    ev = Ctr(s2.enter_context(nc.semaphore("s9bev")))
                    ys = Ctr(s2.enter_context(nc.semaphore("s9bys")))
                    fn = Ctr(s2.enter_context(nc.semaphore("s9bfn")))
                    cc = Ctr(s2.enter_context(nc.semaphore("s9bcc")))
                    with nc.Block() as b:
                        @b.gpsimd
                        def _(gp):
                            gp.dma_start(out=w2m[:].rearrange("p (c o) -> p c o", o=D),
                                         in_=w_fc2t.rearrange("(c p) o -> p c o", p=128)).then_inc(ld.sem, 16)
                            gp.dma_start(out=f2br[:], in_=w_fc2b[:, :]).then_inc(ld.sem, 16)
                            gp.dma_start(out=lsr[:], in_=w_ls[:, :]).then_inc(ld.sem, 16)
                            ld.v += 48
                            ld.wait(gp)
                    nc.all_engine_barrier()
                    with nc.Block() as b:
                        @b.tensor
                        def _(t):
                            t.matmul(ps2[:, 0:512], ones1f[:], lsr[0:1, 0:512], start=True, stop=True)
                            t.matmul(ps0[:, 0:512], ones1f[:], lsr[0:1, 512:1024], start=True, stop=True)
                    nc.all_engine_barrier()
                    with nc.Block() as b:
                        @b.vector
                        def _(v):
                            v.tensor_copy(lsB[:, 0:512], ps2[:, 0:512])
                            v.tensor_copy(lsB[:, 512:1024], ps0[:, 0:512])
                    nc.all_engine_barrier()
                    base_ld = ld.v
                    with nc.Block() as b:
                        @b.sync
                        def _(sync):
                            for tt in range(NTE):
                                if tt >= 2:
                                    sync.wait_ge(pj.sem, (tt - 1) * 2)
                                sync.dma_start(
                                    out=h2t[:, (tt % 2) * 32 * 128:(tt % 2 + 1) * 32 * 128].rearrange("p (c t) -> p c t", t=128),
                                    in_=d_hmoe.ap().rearrange("(c p) t -> p c t", p=128)[:, :, tt * 128:(tt + 1) * 128]
                                ).then_inc(ld.sem, 16)
                                ld.v += 16
                                if tt >= 2:
                                    sync.wait_ge(fn.sem, (tt - 1) * 4)
                                sync.dma_start(out=tokt[:, (tt % 2) * D:(tt % 2) * D + D],
                                               in_=d_tok.ap()[tt * 128:(tt + 1) * 128, :]).then_inc(ld.sem, 16)
                                ld.v += 16

                        @b.tensor
                        def _(t):
                            for tt in range(NTE):
                                t.wait_ge(ld.sem, base_ld + tt * 32 + 16)
                                for os2 in range(2):
                                    k = tt * 2 + os2
                                    if k >= 2:
                                        t.wait_ge(ev.sem, k - 1)
                                    pb = ps0 if k % 2 == 0 else ps1
                                    for dc in range(32):
                                        t.matmul(pb[:, 0:512],
                                                 h2t[:, (tt % 2) * 32 * 128 + dc * 128:(tt % 2) * 32 * 128 + dc * 128 + 128],
                                                 w2m[:, dc * D + os2 * 512: dc * D + os2 * 512 + 512],
                                                 start=(dc == 0), stop=False)
                                    mmi = t.matmul(pb[:, 0:512], ones1b[:], f2br[:, os2 * 512:(os2 + 1) * 512],
                                                   start=False, stop=True)
                                    pj.inc(mmi)

                        @b.vector
                        def _(v):
                            for tt in range(NTE):
                                v.wait_ge(ld.sem, base_ld + (tt + 1) * 32)
                                fn.inc(v.tensor_copy(tokf[:], tokt[:, (tt % 2) * D:(tt % 2) * D + D]))
                                fn.wait(v)
                                if tt >= 1:
                                    v.wait_ge(ys.sem, tt * 16)
                                for os2 in range(2):
                                    k = tt * 2 + os2
                                    v.wait_ge(pj.sem, k + 1)
                                    pb = ps0 if k % 2 == 0 else ps1
                                    fn.inc(v.tensor_mul(yst[:, os2 * 512:(os2 + 1) * 512], pb[:, 0:512],
                                                        lsB[:, os2 * 512:(os2 + 1) * 512]))
                                    fn.wait(v)
                                    ev.inc(v.tensor_add(yst[:, os2 * 512:(os2 + 1) * 512],
                                                        yst[:, os2 * 512:(os2 + 1) * 512],
                                                        tokf[:, os2 * 512:(os2 + 1) * 512]))
                                fn.inc(v.tensor_copy(ysb[:], yst[:]))
                                fn.wait(v)

                        @b.gpsimd
                        def _(gp):
                            for tt in range(NTE):
                                gp.wait_ge(fn.sem, (tt + 1) * 4)
                                ys.inc(gp.dma_start(out=d_ymoe.ap()[tt * 128:(tt + 1) * 128, :], in_=ysb[:]), 16)
                            gp.wait_ge(ys.sem, ys.v)
                    nc.all_engine_barrier()
                    with nc.Block() as b:
                        @b.gpsimd
                        def _(gp):
                            gp.collective_compute(
                                "AllGather", ALU.bypass,
                                replica_groups=[list(range(8))],
                                ins=[d_ymoe.ap()], outs=[c_y_out.ap()],
                            ).then_inc(cc.sem)
                            cc.v += 1
                            cc.wait(gp)
                            if d_ymdbg is not None:
                                gp.dma_start(out=d_ymdbg.ap(), in_=d_ymoe.ap()).then_inc(ld.sem, 16)
                                ld.v += 16
                                ld.wait(gp)
                    nc.all_engine_barrier()

        if want("s10"):
            with contextlib.ExitStack() as st:
                tr1 = st.enter_context(nc.sbuf_tensor("s10_tr1", [128, NT], I32))
                tr2 = st.enter_context(nc.sbuf_tensor("s10_tr2", [128, NT], I32))
                tga = st.enter_context(nc.sbuf_tensor("s10_tga", [128, NT * 2], F32))
                tgb = st.enter_context(nc.sbuf_tensor("s10_tgb", [128, NT * 2], F32))
                io1 = st.enter_context(nc.sbuf_tensor("s10_io1", [128, NT], I32))
                io2 = st.enter_context(nc.sbuf_tensor("s10_io2", [128, NT], I32))
                wc1 = st.enter_context(nc.sbuf_tensor("s10_wc1", [128, NT], F32))
                wc2 = st.enter_context(nc.sbuf_tensor("s10_wc2", [128, NT], F32))
                g1 = st.enter_context(nc.sbuf_tensor("s10_g1", [128, 2 * D], BF16))
                g2 = st.enter_context(nc.sbuf_tensor("s10_g2", [128, 2 * D], BF16))
                x1t = st.enter_context(nc.sbuf_tensor("s10_x1t", [128, 2 * D], F32))
                a1t = st.enter_context(nc.sbuf_tensor("s10_a1t", [128, 2 * D], F32))
                acc = st.enter_context(nc.sbuf_tensor("s10_acc", [128, 2 * D], F32))
                tmp = st.enter_context(nc.sbuf_tensor("s10_tmp", [128, D], F32))
                ld = Ctr(st.enter_context(nc.semaphore("s10ld")))
                gl = Ctr(st.enter_context(nc.semaphore("s10gl")))
                fn = Ctr(st.enter_context(nc.semaphore("s10fn")))
                yo = Ctr(st.enter_context(nc.semaphore("s10yo")))

                with nc.Block() as b:
                    @b.sync
                    def _(sync):
                        ld.inc(sync.dma_start(out=tr1[:], in_=w_tokrows1[:, :]), 16)
                        ld.inc(sync.dma_start(out=tr2[:], in_=w_tokrows2[:, :]), 16)

                    @b.gpsimd
                    def _(gp):
                        ld.wait(gp)
                        for tt in range(NT):
                            gl.inc(gp.indirect_dma_start(
                                out=tga[:].rearrange("p (t f) -> p t f", f=2)[:, tt, :], out_offset=None,
                                in_=d_T.ap(), in_offset=IndirectOffsetOnAxis(tr1[:, tt:tt + 1], 0)), 16)
                            gl.inc(gp.indirect_dma_start(
                                out=tgb[:].rearrange("p (t f) -> p t f", f=2)[:, tt, :], out_offset=None,
                                in_=d_T.ap(), in_offset=IndirectOffsetOnAxis(tr2[:, tt:tt + 1], 0)), 16)
                        gl.wait(gp)
                nc.all_engine_barrier()
                with nc.Block() as b:
                    @b.vector
                    def _(v):
                        fn.inc(v.tensor_copy(wc1[:], tga[:].rearrange("p (t f) -> p t f", f=2)[:, :, 0]))
                        fn.inc(v.tensor_copy(wc2[:], tgb[:].rearrange("p (t f) -> p t f", f=2)[:, :, 0]))
                        fn.inc(v.tensor_copy(io1[:], tga[:].rearrange("p (t f) -> p t f", f=2)[:, :, 1]))
                        fn.inc(v.tensor_copy(io2[:], tgb[:].rearrange("p (t f) -> p t f", f=2)[:, :, 1]))
                        fn.wait(v)
                nc.all_engine_barrier()
                if d_cydbg is not None:
                    with nc.Block() as b:
                        @b.sync
                        def _(sync):
                            ld.inc(sync.dma_start(out=d_cydbg.ap(), in_=c_y_out.ap()), 16)
                            sync.dma_start(out=d_iodbg.ap()[:, 0:16], in_=wc1[:]).then_inc(ld.sem, 16)
                            sync.dma_start(out=d_iodbg.ap()[:, 16:32], in_=wc2[:]).then_inc(ld.sem, 16)
                            ld.v += 32
                            ld.wait(sync)

                        @b.gpsimd
                        def _(gp):
                            gp.dma_start(out=d_iodbg.ap()[:, 32:48], in_=io1[:]).then_inc(ld.sem, 16)
                            gp.dma_start(out=d_iodbg.ap()[:, 48:64], in_=io2[:]).then_inc(ld.sem, 16)
                            ld.v += 32
                            ld.wait(gp)
                    nc.all_engine_barrier()
                glb = gl.v
                ldb = ld.v
                with nc.Block() as b:
                    @b.gpsimd
                    def _(gp):
                        for tt in range(NT):
                            par = tt % 2
                            if tt >= 2:
                                gp.wait_ge(fn.sem, 4 + (tt - 1) * 5)
                            gl.inc(gp.indirect_dma_start(
                                out=g1[:, par * D:(par + 1) * D], out_offset=None,
                                in_=c_y_out.ap(), in_offset=IndirectOffsetOnAxis(io1[:, tt:tt + 1], 0)), 16)
                            gl.inc(gp.indirect_dma_start(
                                out=g2[:, par * D:(par + 1) * D], out_offset=None,
                                in_=c_y_out.ap(), in_offset=IndirectOffsetOnAxis(io2[:, tt:tt + 1], 0)), 16)

                    @b.sync
                    def _(sync):
                        for tt in range(NT):
                            par = tt % 2
                            if tt >= 2:
                                sync.wait_ge(fn.sem, 4 + (tt - 1) * 5)
                            sync.dma_start(out=x1t[:, par * D:(par + 1) * D],
                                           in_=d_x1.ap()[tt * 128:(tt + 1) * 128, :]).then_inc(ld.sem, 16)
                            sync.dma_start(out=a1t[:, par * D:(par + 1) * D],
                                           in_=d_attn1.ap()[tt * 128:(tt + 1) * 128, :]).then_inc(ld.sem, 16)
                            ld.v += 32

                    @b.vector
                    def _(v):
                        for tt in range(NT):
                            par = tt % 2
                            v.wait_ge(gl.sem, glb + (tt + 1) * 32)
                            v.wait_ge(ld.sem, ldb + (tt + 1) * 32)
                            if tt >= 2:
                                v.wait_ge(yo.sem, (tt - 1) * 16)
                            fn.inc(v.tensor_scalar(acc[:, par * D:(par + 1) * D], g1[:, par * D:(par + 1) * D],
                                                   wc1[:, tt:tt + 1], None, ALU.mult))
                            fn.inc(v.tensor_scalar(tmp[:], g2[:, par * D:(par + 1) * D],
                                                   wc2[:, tt:tt + 1], None, ALU.mult))
                            fn.wait(v)
                            fn.inc(v.tensor_add(acc[:, par * D:(par + 1) * D],
                                                acc[:, par * D:(par + 1) * D], tmp[:]))
                            fn.wait(v)
                            fn.inc(v.tensor_add(acc[:, par * D:(par + 1) * D],
                                                acc[:, par * D:(par + 1) * D], x1t[:, par * D:(par + 1) * D]))
                            fn.wait(v)
                            fn.inc(v.tensor_add(acc[:, par * D:(par + 1) * D],
                                                acc[:, par * D:(par + 1) * D], a1t[:, par * D:(par + 1) * D]))
                            fn.wait(v)

                    @b.scalar
                    def _(s3):
                        for tt in range(NT):
                            par = tt % 2
                            s3.wait_ge(fn.sem, 4 + (tt + 1) * 5)
                            yo.inc(s3.dma_start(out=y_out[tt * 128:(tt + 1) * 128, :],
                                                in_=acc[:, par * D:(par + 1) * D]), 16)
                        s3.wait_ge(yo.sem, yo.v)
                nc.all_engine_barrier()
    return nc


def host_l0_weights(inputs):
    import ml_dtypes
    Wq = np.asarray(inputs["L0_attn_Wq"])
    Wk = np.asarray(inputs["L0_attn_Wk"])
    Wv = np.asarray(inputs["L0_attn_Wv"])
    perm = []
    for g in range(4):
        for j in range(4):
            h = g + 4 * j
            perm.extend(range(h * 32, h * 32 + 32))
    l0wq = np.ascontiguousarray(Wq[:, perm])
    out = {
        "l0wq": l0wq,
        "l0wk": np.ascontiguousarray(Wk),
        "l0wv": np.ascontiguousarray(Wv),
        "l0wp": np.ascontiguousarray(inputs["L0_attn_Wp"]),
        "l0bp": np.ascontiguousarray(inputs["L0_attn_bp"]).reshape(1, D),
    }
    bf = ml_dtypes.bfloat16
    tk = np.arange(T)
    posk = np.stack([tk % 128, 128.0 * (tk // 128), tk % 128, 128.0 * (tk // 128),
                     np.ones(T)]).astype(np.float32)
    out["posk"] = posk.astype(bf)
    slope4 = np.zeros((L0_H, 5, T), np.float32)
    for h in range(L0_H):
        sl = np.float32(SLOPES[h])
        shi = np.float32(bf(sl))
        slo = np.float32(bf(np.float32(sl - shi)))
        slope4[h, 0, :] = shi
        slope4[h, 1, :] = shi
        slope4[h, 2, :] = slo
        slope4[h, 3, :] = slo
        slope4[h, 4, :] = -sl * (128.0 * (tk // 128) + 127.0)
    out["slope4"] = slope4.astype(bf)
    onesblk = np.zeros((128, 4), np.float32)
    for j in range(4):
        onesblk[32 * j:32 * j + 32, j] = 1.0
    out["onesblk"] = onesblk
    m01 = (np.arange(128)[:, None] <= np.arange(128)[None, :]).astype(np.float32)
    out["mask01"] = m01.astype(bf)
    out["l0w1t"] = np.ascontiguousarray(np.asarray(inputs["L0_ffn_w1"]).T)
    out["l0cw"] = np.ascontiguousarray(np.asarray(inputs["L0_ffn_conv_w"]).reshape(4096, 3))
    out["l0cb"] = np.ascontiguousarray(np.asarray(inputs["L0_ffn_conv_b"]).reshape(32, 128).T)
    out["l0w2t"] = np.ascontiguousarray(np.asarray(inputs["L0_ffn_w2"]).T)
    out["l1wqt"] = np.ascontiguousarray(np.asarray(inputs["L1_attn_Wq"]).T)
    out["l1wkt"] = np.ascontiguousarray(np.asarray(inputs["L1_attn_Wk"]).T)
    out["l1wvt"] = np.ascontiguousarray(np.asarray(inputs["L1_attn_Wv"]).T)
    out["l1wot"] = np.ascontiguousarray(np.asarray(inputs["L1_attn_Wo"]).T)
    out["l1bq"] = np.ascontiguousarray(np.asarray(inputs["L1_attn_bq"]).reshape(ND, 128).T)
    out["l1bk"] = np.ascontiguousarray(np.asarray(inputs["L1_attn_bk"]).reshape(ND, 128).T)
    out["l1bv"] = np.ascontiguousarray(np.asarray(inputs["L1_attn_bv"]).reshape(1, D))
    out["l1bo"] = np.ascontiguousarray(np.asarray(inputs["L1_attn_bo"]).reshape(1, D))
    sb = np.zeros((1, 4), np.int32); sb[0, :3] = np.asarray(inputs["L1_attn_src_blocks"]); sb[0, 3] = 999
    tb = np.zeros((1, 4), np.int32); tb[0, :3] = np.asarray(inputs["L1_attn_tgt_blocks"]); tb[0, 3] = 999
    out["l1src"] = sb
    out["l1tgt"] = tb
    o2 = np.zeros((128, 2), np.float32)
    o2[0:64, 0] = 1.0
    o2[64:128, 1] = 1.0
    out["ones64"] = o2
    out["rwt"] = np.ascontiguousarray(np.asarray(inputs["L1_ffn_router_w"]).T)
    out["rb"] = np.ascontiguousarray(np.asarray(inputs["L1_ffn_router_b"]).reshape(1, 8))
    import ml_dtypes as _md
    iu = np.arange(128)
    out["utri"] = (iu[:, None] <= iu[None, :]).astype(np.float32).astype(_md.bfloat16)
    out["usf"] = (iu[:, None] < iu[None, :]).astype(np.float32)
    return out


def host_core_inputs(inputs, c):
    m = {}
    m["fc1t"] = np.ascontiguousarray(np.asarray(inputs["L1_ffn_fc1_w"])[c].T)
    m["fc1b"] = np.ascontiguousarray(np.asarray(inputs["L1_ffn_fc1_b"])[c].reshape(32, 128).T)
    m["fc2t"] = np.ascontiguousarray(np.asarray(inputs["L1_ffn_fc2_w"])[c].T)
    m["fc2b"] = np.ascontiguousarray(np.asarray(inputs["L1_ffn_fc2_b"])[c].reshape(1, D))
    m["ls"] = np.ascontiguousarray(np.asarray(inputs["L1_ffn_layer_scale"])[c].reshape(1, D))
    er = (c * CAP_PAD + np.arange(CAP_PAD)).astype(np.int32)
    m["exprows"] = np.ascontiguousarray(er.reshape(20, 128).T)
    t_ids = c * T + np.arange(T)
    p, cc_ = t_ids % 128, (t_ids % T) // 128 + (t_ids // T) * 0
    pg, cg = t_ids % 128, t_ids // 128
    m["tokrows1"] = np.ascontiguousarray((pg * 256 + 0 * 128 + cg).astype(np.int32).reshape(NT, 128).T)
    m["tokrows2"] = np.ascontiguousarray((pg * 256 + 1 * 128 + cg).astype(np.int32).reshape(NT, 128).T)
    oc = (c * 32 + np.arange(32)).astype(np.int32)
    m["owncols"] = np.ascontiguousarray(np.concatenate([oc, oc + 256]).reshape(64, 1))
    m["tv2"] = np.ascontiguousarray(
        (np.arange(128)[:, None] + 128.0 * (oc[None, :] % 128)).astype(np.float32))
    return m


def kernel(**inputs):
    x = np.ascontiguousarray(inputs["L0_attn_x"])
    nc = build_program()
    shared = host_l0_weights(inputs)
    in_maps = []
    for c in range(8):
        m = {"x": np.ascontiguousarray(x[c])}
        m.update(shared)
        m.update(host_core_inputs(inputs, c))
        in_maps.append(m)
    res = run_bass_kernel_spmd(nc, in_maps, core_ids=list(range(8)))
    return np.stack([res.results[c]["y"] for c in range(8)], axis=0)
